# version 09
# speedup vs baseline: 1.0132x; 1.0011x over previous
import contextlib
import numpy as np
import concourse.bass as bass
import concourse.mybir as mybir
from concourse.bass_utils import run_bass_kernel_spmd
from concourse.aot_env import allow_ici_all_to_all

F32 = mybir.dt.float32
F16 = mybir.dt.float16
BF16 = mybir.dt.bfloat16
I32 = mybir.dt.int32
U8 = mybir.dt.uint8
AF = mybir.ActivationFunctionType
ALU = mybir.AluOpType
AX = mybir.AxisListType

B, T, H = 2, 4096, 2048
TPC = 1024
E = 4096
N_SSM = 16
I1 = 8192
LN_EPS = 1e-5
L0_EPS = 1e-6
L1_EPS = 1e-5
ISQD = 0.08838834764831845
ALL8 = [list(range(8))]
GRP4 = [[0, 1, 2, 3], [4, 5, 6, 7]]
ALL_STAGES = ("l0ffn1", "mamba", "l0ffn2", "l1ffn1", "attn", "l1ffn2")


class SemT:
    def __init__(self, h):
        self.h = h
        self.n = 0

    def inc(self, k=1):
        self.n += k
        return self.n


class Ctx:
    pass


_UID = [0]


def _u(name):
    _UID[0] += 1
    return f"{name}_{_UID[0]}"


def build_program(stages=ALL_STAGES):
    nc = bass.Bass("TRN2", target_bir_lowering=False, debug=False)
    t = {}

    def ein(name, shape, dtype):
        t[name] = nc.dram_tensor(name, shape, dtype, kind="ExternalInput").ap()

    ein("x_in", [TPC, H], F32)
    ein("gsel", [128, 4], F32)
    ein("pmask", [128, 1], F32)
    ein("sel32", [32, 32 * 128], BF16)
    ein("ident_bf", [128, 128], BF16)
    ein("ident_f", [128, 128], F32)
    ein("mask_diag", [128, 128], F32)
    ein("mask_far", [128, 128], F32)
    ein("gup_pk", [1408, 1024], U8)
    ein("gup_sc", [1408, 16], F16)
    ein("dn_pk", [256, 2816], U8)
    ein("dn_sc", [256, 44], F16)
    ein("wg_sh", [1024, H], F32)
    ein("wu_sh", [1024, H], F32)
    ein("wd_sh", [256, I1], F32)
    ein("win_sh", [1024, H], F32)
    ein("wout_sh", [256, E], F32)
    ein("xproj", [160, E], F32)
    ein("dtw_sh", [512, 128], F32)
    ein("dtb_sh", [512], F32)
    ein("alog_sh", [512, N_SSM], F32)
    ein("d_sh", [512], F32)
    ein("convw", [E, 4], F32)
    ein("convb", [E], F32)
    ein("wq_d", [H, 2048], F32)
    ein("wk_d", [H, 512], F32)
    ein("wv_d", [H, 512], F32)
    ein("wo_d", [2048, 2048], F32)
    ein("bo_d", [2048], F32)
    t["out_x"] = nc.dram_tensor("out_x", [TPC, H], F32, kind="ExternalOutput").ap()

    def intr(name, shape, dtype=BF16):
        t[name] = nc.dram_tensor(name, shape, dtype).ap()

    intr("gu_my", [1, 11 * 128 * 2048])
    intr("gu_all", [8, 11 * 128 * 2048])
    intr("dnw_my", [1, 2 * 128 * 5632])
    intr("dnw_all", [8, 2 * 128 * 5632])
    intr("wgq_my", [1, 16 * 128 * 1024])
    intr("wgq_all", [8, 16 * 128 * 1024])
    intr("wuq_my", [1, 16 * 128 * 1024])
    intr("wuq_all", [8, 16 * 128 * 1024])
    intr("wdq_my", [1, 2 * 128 * 8192])
    intr("wdq_all", [8, 2 * 128 * 8192])
    intr("win_my", [1, 8 * 128 * 2048])
    intr("win_all", [8, 8 * 128 * 2048])
    intr("wout_my", [1, 2 * 128 * 4096])
    intr("wout_all", [8, 2 * 128 * 4096])
    intr("absum_my", [1, 4], F32)
    intr("absum_all", [1, 4], F32)
    intr("xhalo_my", [1, H * 3])
    intr("xhalo_all", [4, H * 3])
    intr("ssm_my", [1, 160 * 1024], F32)
    intr("ssm_all", [8, 160 * 1024], F32)
    intr("h_a2a_in", [8, 512 * 1024])
    intr("h_a2a_out", [8, 512 * 1024])
    intr("y_a2a_in", [8, 512 * 1024])
    intr("y_a2a_out", [8, 512 * 1024])
    intr("sg_d", [1, E * 1024])
    intr("h_d", [1, I1 * 1024])
    intr("hq_d", [1, I1 * 1024])
    intr("kv_my", [1, 2 * 512 * 1024])
    intr("kv_all", [4, 2 * 512 * 1024])

    with contextlib.ExitStack() as stack:
        with nc.Block() as blk:
            _emit(nc, blk, stack, t, stages)
    return nc


def _emit(nc, blk, stack, t, stages):
    c = Ctx()
    c.nc, c.blk, c.t = nc, blk, t
    c.stages = stages
    sb = lambda name, shape, dt: stack.enter_context(nc.sbuf_tensor(name, shape, dt))

    for nm in ["s_dw", "s_dx", "s_dg", "s_t", "s_v", "s_s", "s_g", "s_cc", "s_ev"]:
        setattr(c, nm, SemT(stack.enter_context(nc.semaphore(nm))))
    c.s_w = [SemT(stack.enter_context(nc.semaphore(f"s_w{i}"))) for i in range(6)]
    s_dw, s_dx, s_dg = c.s_dw, c.s_dx, c.s_dg
    s_t, s_v, s_s, s_g, s_cc, s_ev = c.s_t, c.s_v, c.s_s, c.s_g, c.s_cc, c.s_ev

    c.x_res = sb("x_res", [128, 16, 1024], F32)
    c.ones_bf = sb("ones_bf", [128, 1], BF16)
    c.ones_col_f = sb("ones_col_f", [128, 1], F32)
    c.ones_row_f = sb("ones_row_f", [1, 128], F32)
    c.ident_bf = sb("sb_identb", [128, 128], BF16)
    c.ident_f = sb("sb_identf", [128, 128], F32)
    c.mask_diag = sb("sb_mdiag", [128, 128], F32)
    c.mask_far = sb("sb_mfar", [128, 128], F32)
    c.rowbuf = sb("rowbuf", [1, 2048], F32)
    c.arena = sb("arena", [128, 8192], BF16)
    c.convw_c = sb("convw_c", [128, 32, 4], F32)
    c.convb_c = sb("convb_c", [128, 32], F32)
    c.dtb_c = sb("dtb_c", [128, 4], F32)
    c.aneg_c = sb("aneg_c", [128, 4, 16], F32)
    c.dcol_c = sb("dcol_c", [128, 4], F32)
    c.bo_c = sb("bo_c", [128, 16], F32)
    c.ws_c = sb("ws_c", [128, 8], F32)

    @blk.gpsimd
    def _(g):
        g.memset(c.ones_bf[:], 1.0)
        g.memset(c.ones_col_f[:], 1.0)
        g.memset(c.ones_row_f[:], 1.0).then_inc(s_g.h, 1)
    s_g.inc(1)
    c.G_CONST = s_g.n

    @blk.scalar
    def _(sc):
        with nc.allow_non_contiguous_dma(reason="tiny one-time constant loads"):
            sc.dma_start(out=c.ident_bf[:], in_=t["ident_bf"]).then_inc(s_dx.h, 16)
            sc.dma_start(out=c.ident_f[:], in_=t["ident_f"]).then_inc(s_dx.h, 16)
            sc.dma_start(out=c.mask_diag[:], in_=t["mask_diag"]).then_inc(s_dx.h, 16)
            sc.dma_start(out=c.mask_far[:], in_=t["mask_far"]).then_inc(s_dx.h, 16)
            sc.dma_start(out=c.convw_c[:], in_=t["convw"].rearrange("(j p) k -> p j k", p=128)).then_inc(s_dx.h, 16)
            sc.dma_start(out=c.convb_c[:], in_=t["convb"].rearrange("(j p) -> p j", p=128)).then_inc(s_dx.h, 16)
            sc.dma_start(out=c.dtb_c[:], in_=t["dtb_sh"].rearrange("(j p) -> p j", p=128)).then_inc(s_dx.h, 16)
            sc.dma_start(out=c.aneg_c[:], in_=t["alog_sh"].rearrange("(j p) n -> p j n", p=128)).then_inc(s_dx.h, 16)
            sc.dma_start(out=c.dcol_c[:], in_=t["d_sh"].rearrange("(j p) -> p j", p=128)).then_inc(s_dx.h, 16)
            sc.dma_start(out=c.bo_c[:], in_=t["bo_d"].rearrange("(j p) -> p j", p=128)).then_inc(s_dx.h, 16)
    s_dx.inc(160)
    DXC = s_dx.n

    @blk.scalar
    def _(sc):
        sc.wait_ge(s_dx.h, DXC)
        sc.activation(c.aneg_c[:], c.aneg_c[:], AF.Exp)
        sc.drain()
        sc.mul(c.aneg_c[:], c.aneg_c[:], -1.0).then_inc(s_s.h, 1)
    s_s.inc()
    c.S_CONST = s_s.n

    def fence():
        sems = [(c.s_t, c.s_t.n), (c.s_v, c.s_v.n), (c.s_s, c.s_s.n),
                (c.s_ev, c.s_ev.n), (c.s_g, c.s_g.n), (c.s_dw, c.s_dw.n),
                (c.s_dx, c.s_dx.n), (c.s_dg, c.s_dg.n), (c.s_cc, c.s_cc.n)] + \
               [(w, w.n) for w in c.s_w]

        def emit(e):
            for s, v in sems:
                e.wait_ge(s.h, v)

        @blk.vector
        def _(v):
            emit(v)

        @blk.scalar
        def _(s):
            emit(s)

        @blk.gpsimd
        def _(g):
            emit(g)

        @blk.tensor
        def _(te):
            emit(te)

        @blk.sync
        def _(sy):
            emit(sy)
    c.fence = fence

    def transpose_stream(tag, items, dtype, group=4):
        ident = c.ident_bf if dtype == BF16 else c.ident_f
        with nc.psum_tensor(_u(f"pt_{tag}a"), [128, group * 128], dtype) as psa, \
             nc.psum_tensor(_u(f"pt_{tag}b"), [128, group * 128], dtype) as psb:
            ps = [psa, psb]
            ngrp = (len(items) + group - 1) // group
            T0, E0 = s_t.n, s_ev.n

            @blk.tensor
            def _(te):
                te.wait_ge(s_ev.h, E0)
                for gi in range(ngrp):
                    if gi >= 2:
                        te.wait_ge(s_ev.h, E0 + gi - 1)
                    grp = items[gi * group:(gi + 1) * group]
                    for k, (src, dst) in enumerate(grp):
                        ins = te.transpose(ps[gi % 2][:, 128 * k:128 * (k + 1)], src, ident[:])
                        if k == len(grp) - 1:
                            ins.then_inc(s_t.h, 1)
                    s_t.inc()

            @blk.scalar
            def _(sc):
                for gi in range(ngrp):
                    sc.wait_ge(s_t.h, T0 + gi + 1)
                    grp = items[gi * group:(gi + 1) * group]
                    for k, (src, dst) in enumerate(grp):
                        ins = sc.copy(dst, ps[gi % 2][:, 128 * k:128 * (k + 1)])
                        if k == len(grp) - 1:
                            ins.then_inc(s_ev.h, 1)
                    s_ev.inc()
    c.transpose_stream = transpose_stream

    def emit_round(v, out, vin, qi, qc, e1, n):
        v.tensor_copy(qi[:, 0:n], vin[:, 0:n])
        v.drain()
        v.tensor_copy(qc[:, 0:n], qi[:, 0:n])
        v.drain()
        v.tensor_tensor(vin[:, 0:n], vin[:, 0:n], qc[:, 0:n], ALU.subtract)
        v.drain()
        v.tensor_scalar(e1[:, 0:n], vin[:, 0:n], 0.5, None, ALU.is_ge)
        v.drain()
        v.scalar_tensor_tensor(e1[:, 0:n], vin[:, 0:n], -0.5, e1[:, 0:n],
                               ALU.is_le, ALU.subtract)
        v.drain()
        v.tensor_tensor(out[:, 0:n], qc[:, 0:n], e1[:, 0:n], ALU.subtract)
        v.drain()
    c.emit_round = emit_round

    with nc.sbuf_tensor("xstage", [128, 8, 2048], F32) as xstage:
        @blk.scalar
        def _(sc):
            sc.dma_start(out=xstage[:],
                         in_=t["x_in"].rearrange("(tt p) d -> p tt d", p=128)).then_inc(s_dx.h, 16)
        s_dx.inc(16)
        DXI = s_dx.n

        @blk.tensor
        def _(te):
            te.wait_ge(s_dx.h, DXI)
            te.wait_ge(s_g.h, c.G_CONST)

        items = []
        for tt in range(8):
            for ic in range(16):
                items.append((xstage[:, tt, 128 * ic:128 * (ic + 1)],
                              c.x_res[:, ic, 128 * tt:128 * (tt + 1)]))
        transpose_stream("xin", items, F32)
    c.EV_XLOADED = s_ev.n

    need_l0 = "l0ffn1" in stages or "l0ffn2" in stages
    need_l1 = "l1ffn1" in stages or "l1ffn2" in stages
    need_m = "mamba" in stages

    if need_l0:
        fence()
        with nc.named_scope("preA"):
            _preA(c)
    if need_l1 or need_m:
        fence()
        with nc.named_scope("preB"):
            _preB(c, do_l1=need_l1, do_m=need_m)

    for st in ALL_STAGES:
        if st not in stages:
            continue
        fence()
        with nc.named_scope(st):
            if st in ("l0ffn1", "l0ffn2"):
                _l0ffn(c, use=1 if st == "l0ffn1" else 2)
            elif st == "mamba":
                _mamba(c)
            elif st in ("l1ffn1", "l1ffn2"):
                _l1ffn(c, use=1 if st == "l1ffn1" else 2)
            elif st == "attn":
                _attn(c)

    fence()
    with nc.sbuf_tensor("ostage", [128, 2048], F32) as ostage:
        for tt in range(8):
            items = [(c.x_res[:, ic, 128 * tt:128 * (tt + 1)], ostage[:, 128 * ic:128 * (ic + 1)])
                     for ic in range(16)]
            transpose_stream(f"xo{tt}", items, F32)
            EV = s_ev.n

            @blk.scalar
            def _(sc):
                sc.wait_ge(s_ev.h, EV)
                sc.dma_start(out=t["out_x"][128 * tt:128 * (tt + 1), :],
                             in_=ostage[:]).then_inc(s_dx.h, 16)
            s_dx.inc(16)
            DXO = s_dx.n

            @blk.scalar
            def _(sc):
                sc.wait_ge(s_dx.h, DXO)

    @blk.sync
    def _(sy):
        sy.wait_ge(s_dx.h, s_dx.n)


def _preA(c):
    nc, blk, t = c.nc, c.blk, c.t
    s_dx, s_dg, s_v, s_s, s_t, s_cc, s_ev = c.s_dx, c.s_dg, c.s_v, c.s_s, c.s_t, c.s_cc, c.s_ev

    gu_v = t["gu_my"].rearrange("one (ot p x) -> (one ot) p x", ot=11, p=128)
    dn_v = t["dnw_my"].rearrange("one (ot p x) -> (one ot) p x", ot=2, p=128)

    with nc.sbuf_tensor("pk", [128, 2816], U8) as pk, \
         nc.sbuf_tensor("scl", [128, 44], F32) as scl, \
         nc.sbuf_tensor("sclh", [128, 44], F16) as sclh, \
         nc.sbuf_tensor("valf", [128, 5632], F32) as val, \
         nc.sbuf_tensor("valb", [128, 5632], BF16) as valb, \
         nc.sbuf_tensor("stgA", [128, 5632], BF16) as stg, \
         nc.sbuf_tensor("msk15", [128, 1408], U8) as msk15, \
         nc.sbuf_tensor("sh4", [128, 1408], U8) as sh4, \
         nc.sbuf_tensor("nib8", [128, 1408], U8) as nib8, \
         nc.sbuf_tensor("nibf", [128, 1408], F32) as nibf, \
         nc.sbuf_tensor("mfv", [128, 1408], F32) as mf, \
         nc.sbuf_tensor("ufv", [128, 1408], F32) as uf, \
         nc.sbuf_tensor("sgv", [128, 1408], F32) as sgt:

        @blk.vector
        def _(v):
            v.memset(msk15[:], 15)
            v.memset(sh4[:], 4)
            v.drain()

        tiles = [("gu", ti, 1024, 16, gu_v, ti) for ti in range(11)] + \
                [("dn", ti, 2816, 44, dn_v, ti) for ti in range(2)]
        prev_dg = s_dg.n
        for kind, ti, nw, ngrp, dst_view, ot in tiles:
            pk_src = t["gup_pk"] if kind == "gu" else t["dn_pk"]
            sc_src = t["gup_sc"] if kind == "gu" else t["dn_sc"]

            @blk.scalar
            def _(sc):
                sc.wait_ge(s_v.h, s_v.n)
                sc.dma_start(out=pk[:, 0:nw], in_=pk_src[128 * ti:128 * (ti + 1), :]).then_inc(s_dx.h, 16)
                sc.dma_start(out=sclh[:, 0:ngrp], in_=sc_src[128 * ti:128 * (ti + 1), :]).then_inc(s_dx.h, 16)
            s_dx.inc(32)
            DXT = s_dx.n

            @blk.scalar
            def _(sc):
                sc.wait_ge(s_dx.h, DXT)
                sc.copy(scl[:, 0:ngrp], sclh[:, 0:ngrp])
                sc.drain()
                sc.mul(scl[:, 0:ngrp], scl[:, 0:ngrp], 0.5).then_inc(s_s.h, 1)
            s_s.inc()
            SST = s_s.n

            EV_PREV_TR = s_ev.n

            @blk.vector
            def _(v):
                v.wait_ge(s_s.h, SST)
                v.wait_ge(s_ev.h, EV_PREV_TR)
                nq = nw // 2
                for half in range(2):
                    for cc in range(2):
                        P = slice(nq * cc, nq * (cc + 1))
                        Q = slice(0, nq)
                        if half == 0:
                            v.tensor_tensor(nib8[:, Q], pk[:, P], msk15[:, Q], ALU.bitwise_and)
                        else:
                            v.tensor_tensor(nib8[:, Q], pk[:, P], sh4[:, Q], ALU.logical_shift_right)
                        v.drain()
                        v.tensor_scalar(nibf[:, Q], nib8[:, Q], 0.0, None, ALU.add)
                        v.drain()
                        v.tensor_scalar(sgt[:, Q], nibf[:, Q], 8.0, None, ALU.is_ge)
                        v.drain()
                        v.scalar_tensor_tensor(mf[:, Q], sgt[:, Q], -8.0, nibf[:, Q],
                                               ALU.mult, ALU.add)
                        v.drain()
                        v.scalar_tensor_tensor(uf[:, Q], mf[:, Q], 5.0, mf[:, Q], ALU.is_ge, ALU.add)
                        v.drain()
                        v.scalar_tensor_tensor(uf[:, Q], mf[:, Q], 6.0, uf[:, Q], ALU.is_ge, ALU.add)
                        v.drain()
                        for _i in range(3):
                            v.scalar_tensor_tensor(uf[:, Q], mf[:, Q], 7.0, uf[:, Q], ALU.is_ge, ALU.add)
                            v.drain()
                        v.tensor_scalar(sgt[:, Q], sgt[:, Q], -2.0, 1.0, ALU.mult, ALU.add)
                        v.drain()
                        v.tensor_mul(val[:, 2 * nq * cc + half:2 * nq * (cc + 1):2], uf[:, Q], sgt[:, Q])
                        v.drain()
                for g in range(ngrp):
                    ins = v.tensor_scalar_mul(valb[:, 128 * g:128 * (g + 1)],
                                              val[:, 128 * g:128 * (g + 1)], scl[:, g:g + 1])
                    if g == ngrp - 1:
                        ins.then_inc(s_v.h, 1)
            s_v.inc()
            VT = s_v.n

            DG_PREV = prev_dg

            @blk.scalar
            def _(sc):
                sc.wait_ge(s_dg.h, DG_PREV)

            @blk.tensor
            def _(te):
                te.wait_ge(s_v.h, VT)
            items = [(valb[:, 128 * g:128 * (g + 1)], stg[:, 128 * g:128 * (g + 1)])
                     for g in range(ngrp)]
            c.transpose_stream(f"pa{kind}{ti}", items, BF16)
            EVT = s_ev.n

            @blk.gpsimd
            def _(g):
                g.wait_ge(s_ev.h, EVT)
                g.dma_start(out=dst_view[ot], in_=stg[:, 0:128 * ngrp]).then_inc(s_dg.h, 16)
            s_dg.inc(16)
            prev_dg = s_dg.n

        DGA = s_dg.n

        @blk.gpsimd
        def _(g):
            g.wait_ge(s_dg.h, DGA)
            g.collective_compute("AllGather", ALU.bypass, replica_groups=ALL8,
                                 ins=[t["gu_my"]], outs=[t["gu_all"]]).then_inc(s_cc.h, 1)
            g.collective_compute("AllGather", ALU.bypass, replica_groups=ALL8,
                                 ins=[t["dnw_my"]], outs=[t["dnw_all"]]).then_inc(s_cc.h, 1)
        s_cc.inc(2)
        c.CC_PREA = s_cc.n


def _preB(c, do_l1=True, do_m=True):
    nc, blk, t = c.nc, c.blk, c.t
    s_dx, s_dg, s_v, s_s, s_t, s_cc, s_ev = c.s_dx, c.s_dg, c.s_v, c.s_s, c.s_t, c.s_cc, c.s_ev

    win_v = t["win_my"].rearrange("one (ot p x) -> (one ot) p x", ot=8, p=128)
    wout_v = t["wout_my"].rearrange("one (ot p x) -> (one ot) p x", ot=2, p=128)
    wdq_v = t["wdq_my"].rearrange("one (ot p x) -> (one ot) p x", ot=2, p=128)
    wgq_v = t["wgq_my"].rearrange("one (ic p x) -> (one ic) p x", ic=16, p=128)
    wuq_v = t["wuq_my"].rearrange("one (ic p x) -> (one ic) p x", ic=16, p=128)

    with nc.sbuf_tensor("wfB", [128, 8192], F32) as wf, \
         nc.sbuf_tensor("weB", [128, 8192], F32) as we, \
         nc.sbuf_tensor("wbB", [128, 8192], BF16) as wb, \
         nc.sbuf_tensor("stgB", [128, 8192], BF16) as stg, \
         nc.sbuf_tensor("paccB", [128, 4], F32) as pacc, \
         nc.sbuf_tensor("rsumB", [128, 4], F32) as rsum, \
         nc.sbuf_tensor("absrow", [1, 8], F32) as absrow:

        if do_l1:
            @blk.vector
            def _(v):
                v.memset(rsum[:], 0.0)
                v.drain()

            jobs = [(t["wg_sh"], 8, 2048, 0), (t["wu_sh"], 8, 2048, 1), (t["wd_sh"], 2, 8192, 2)]
            for src, ntile, cols, ci in jobs:
                for ti in range(ntile):
                    @blk.scalar
                    def _(sc):
                        sc.wait_ge(s_v.h, s_v.n)
                        sc.dma_start(out=wf[:, 0:cols],
                                     in_=src[128 * ti:128 * (ti + 1), :]).then_inc(s_dx.h, 16)
                    s_dx.inc(16)
                    DD = s_dx.n

                    @blk.vector
                    def _(v):
                        v.wait_ge(s_dx.h, DD)
                        v.reduce_sum(pacc[:, 0:1], wf[:, 0:cols], AX.X, apply_absolute_value=True)
                        v.drain()
                        v.tensor_add(rsum[:, ci:ci + 1], rsum[:, ci:ci + 1],
                                     pacc[:, 0:1]).then_inc(s_v.h, 1)
                        v.drain()
                    s_v.inc()

            with nc.psum_tensor("pabs", [1, 4], F32) as pabs:
                VV = s_v.n

                @blk.tensor
                def _(te):
                    te.wait_ge(s_v.h, VV)
                    te.matmul(pabs[0:1, 0:4], c.ones_col_f[:, 0:1], rsum[:, 0:4],
                              start=True, stop=True).then_inc(s_t.h, 1)
                s_t.inc()
                TT0 = s_t.n

                @blk.vector
                def _(v):
                    v.wait_ge(s_t.h, TT0)
                    v.tensor_copy(absrow[:, 0:4], pabs[0:1, 0:4]).then_inc(s_v.h, 1)
                s_v.inc()
                VV1 = s_v.n

            @blk.scalar
            def _(sc):
                sc.wait_ge(s_v.h, VV1)
                sc.dma_start(out=t["absum_my"], in_=absrow[:, 0:4]).then_inc(s_dx.h, 16)
            s_dx.inc(16)
            DD1 = s_dx.n

            @blk.gpsimd
            def _(g):
                g.wait_ge(s_dx.h, DD1)
                g.collective_compute("AllReduce", ALU.add, replica_groups=ALL8,
                                     ins=[t["absum_my"]], outs=[t["absum_all"]]).then_inc(s_cc.h, 1)
            s_cc.inc()
            CCW = s_cc.n

            @blk.scalar
            def _(sc):
                sc.wait_ge(s_cc.h, CCW)
                sc.dma_start(out=absrow[:, 4:8], in_=t["absum_all"]).then_inc(s_dx.h, 16)
            s_dx.inc(16)
            DD2 = s_dx.n

            @blk.vector
            def _(v):
                v.wait_ge(s_dx.h, DD2)
                v.tensor_scalar(absrow[0:1, 0:3], absrow[0:1, 4:7], 1.0 / (I1 * H), L1_EPS,
                                ALU.mult, ALU.add)
                v.drain()
                v.tensor_scalar(absrow[0:1, 4:7], absrow[0:1, 0:3], 0.5, None, ALU.mult)
                v.drain()
                v.tensor_scalar(absrow[0:1, 7:8], absrow[0:1, 2:3], 0.5, None, ALU.mult)
                v.drain()
                v.engine_nop().then_inc(s_v.h, 1)
            s_v.inc()
            VWS = s_v.n

            with nc.psum_tensor("pws", [128, 8], F32) as pws:
                @blk.tensor
                def _(te):
                    te.wait_ge(s_v.h, VWS)
                    te.matmul(pws[:, 0:8], c.ones_row_f[0:1, :], absrow[0:1, 0:8],
                              start=True, stop=True).then_inc(s_t.h, 1)
                s_t.inc()
                TT1 = s_t.n

                @blk.vector
                def _(v):
                    v.wait_ge(s_t.h, TT1)
                    v.tensor_copy(c.ws_c[:, 0:8], pws[:, 0:8]).then_inc(s_v.h, 1)
                    v.drain()
                s_v.inc()
                c.V_WS = s_v.n

        def shard_tiles(tag, src, ntile, cols, writeback, quant_col=None):
            nic = cols // 128
            nonlocal_prev = [s_dg.n]
            for ti in range(ntile):
                @blk.scalar
                def _(sc):
                    sc.wait_ge(s_v.h, s_v.n)
                    sc.dma_start(out=wf[:, 0:cols],
                                 in_=src[128 * ti:128 * (ti + 1), :]).then_inc(s_dx.h, 16)
                s_dx.inc(16)
                DQ = s_dx.n
                EVP = s_ev.n

                if quant_col is not None:
                    @blk.vector
                    def _(v):
                        v.wait_ge(s_dx.h, DQ)
                        v.wait_ge(s_v.h, c.V_WS)
                        v.wait_ge(s_ev.h, EVP)
                        thr = c.ws_c[:, 4 + quant_col:5 + quant_col]
                        v.tensor_scalar(we[:, 0:cols], wf[:, 0:cols], thr, None, ALU.is_ge)
                        v.drain()
                        v.tensor_scalar(wf[:, 0:cols], wf[:, 0:cols], -1.0, None, ALU.mult)
                        v.drain()
                        v.tensor_scalar(wf[:, 0:cols], wf[:, 0:cols], thr, None, ALU.is_ge)
                        v.drain()
                        v.tensor_tensor(wb[:, 0:cols], we[:, 0:cols], wf[:, 0:cols],
                                        ALU.subtract).then_inc(s_v.h, 1)
                        v.drain()
                    s_v.inc()
                else:
                    @blk.scalar
                    def _(sc):
                        sc.wait_ge(s_dx.h, DQ)
                        sc.wait_ge(s_ev.h, EVP)
                        sc.copy(wb[:, 0:cols], wf[:, 0:cols]).then_inc(s_v.h, 1)
                    s_v.inc()
                VQ = s_v.n
                DGP = nonlocal_prev[0]

                @blk.scalar
                def _(sc):
                    sc.wait_ge(s_dg.h, DGP)

                @blk.tensor
                def _(te):
                    te.wait_ge(s_v.h, VQ)
                items = [(wb[:, 128 * ic:128 * (ic + 1)], stg[:, 128 * ic:128 * (ic + 1)])
                         for ic in range(nic)]
                c.transpose_stream(f"{tag}{ti}", items, BF16)
                EQ = s_ev.n

                @blk.gpsimd
                def _(g):
                    g.wait_ge(s_ev.h, EQ)
                    writeback(g, ti)
                nonlocal_prev[0] = s_dg.n

        def wb_rhs(dst_view):
            def wb(g, ti):
                for ic in range(16):
                    g.dma_start(out=dst_view[ic][:, 128 * ti:128 * (ti + 1)],
                                in_=stg[:, 128 * ic:128 * (ic + 1)]).then_inc(c.s_dg.h, 16)
                c.s_dg.inc(16 * 16)
            return wb

        def wb_strip(dst_view, cols):
            def wb(g, ti):
                g.dma_start(out=dst_view[ti], in_=stg[:, 0:cols]).then_inc(c.s_dg.h, 16)
                c.s_dg.inc(16)
            return wb

        names = []
        if do_l1:
            shard_tiles("qg", t["wg_sh"], 8, 2048, wb_rhs(wgq_v), quant_col=0)
            shard_tiles("qu", t["wu_sh"], 8, 2048, wb_rhs(wuq_v), quant_col=1)
            shard_tiles("qd", t["wd_sh"], 2, 8192, wb_strip(wdq_v, 8192), quant_col=2)
            names += ["wgq", "wuq", "wdq"]
        if do_m:
            shard_tiles("ci", t["win_sh"], 8, 2048, wb_strip(win_v, 2048))
            shard_tiles("co", t["wout_sh"], 2, 4096, wb_strip(wout_v, 4096))
            names += ["win", "wout"]

        DGB = s_dg.n

        @blk.gpsimd
        def _(g):
            g.wait_ge(s_dg.h, DGB)
            for nm in names:
                g.collective_compute("AllGather", ALU.bypass, replica_groups=ALL8,
                                     ins=[t[nm + "_my"]], outs=[t[nm + "_all"]]).then_inc(s_cc.h, 1)
        s_cc.inc(len(names))
        c.CC_PREB = s_cc.n

def _layernorm(c, cols, xout, ocols=None, pre_wait=None, w=512):
    nc, blk = c.nc, c.blk
    s_t, s_v, s_s = c.s_t, c.s_v, c.s_s
    x_res, rowbuf = c.x_res, c.rowbuf
    arena_v = c.arena[:].rearrange("p (a b) -> p a b", a=16)
    if ocols is None:
        ocols = slice(0, cols.stop - cols.start)
    n = cols.stop - cols.start

    with nc.psum_tensor(_u("ln_s"), [1, 512], F32) as ps_s, \
         nc.psum_tensor(_u("ln_q"), [1, 512], F32) as ps_q, \
         nc.psum_tensor(_u("ln_b"), [128, 1024], F32) as ps_b:
        for h2 in range(n // w):
            sub = slice(cols.start + w * h2, cols.start + w * (h2 + 1))
            osub = slice(ocols.start + w * h2, ocols.start + w * (h2 + 1))

            VA = s_v.n

            @blk.scalar
            def _(sc):
                if pre_wait is not None and h2 == 0:
                    sc.wait_ge(*pre_wait)
                sc.wait_ge(s_v.h, VA)
                for ic in range(16):
                    ins = sc.copy(arena_v[:, ic, 0:w], x_res[:, ic, sub])
                    if ic == 15:
                        ins.then_inc(s_s.h, 1)
            s_s.inc()
            SL = s_s.n

            @blk.tensor
            def _(te):
                te.wait_ge(s_s.h, SL)
                for ic in range(16):
                    ins = te.matmul(ps_s[0:1, 0:w], c.ones_bf[:, 0:1], arena_v[:, ic, 0:w],
                                    start=(ic == 0), stop=(ic == 15))
                    if ic == 15:
                        ins.then_inc(s_t.h, 1)
            s_t.inc()
            TL1 = s_t.n

            @blk.scalar
            def _(sc):
                sc.wait_ge(s_t.h, TL1)
                for ic in range(16):
                    ins = sc.square(arena_v[:, ic, 0:w], x_res[:, ic, sub])
                    if ic == 15:
                        ins.then_inc(s_s.h, 1)
            s_s.inc()
            SL2 = s_s.n

            @blk.tensor
            def _(te):
                te.wait_ge(s_s.h, SL2)
                for ic in range(16):
                    ins = te.matmul(ps_q[0:1, 0:w], c.ones_bf[:, 0:1], arena_v[:, ic, 0:w],
                                    start=(ic == 0), stop=(ic == 15))
                    if ic == 15:
                        ins.then_inc(s_t.h, 1)
            s_t.inc()
            TL2 = s_t.n

            @blk.vector
            def _(v):
                v.wait_ge(s_t.h, TL2)
                v.tensor_scalar_mul(rowbuf[0:1, 0:w], ps_s[0:1, 0:w], 1.0 / 2048.0)
                v.tensor_scalar_mul(rowbuf[0:1, 512:512 + w], ps_q[0:1, 0:w], 1.0 / 2048.0)
                v.drain()
                v.tensor_mul(rowbuf[0:1, 1024:1024 + w], rowbuf[0:1, 0:w], rowbuf[0:1, 0:w])
                v.drain()
                v.tensor_sub(rowbuf[0:1, 512:512 + w], rowbuf[0:1, 512:512 + w], rowbuf[0:1, 1024:1024 + w])
                v.drain()
                v.tensor_scalar_add(rowbuf[0:1, 512:512 + w], rowbuf[0:1, 512:512 + w],
                                    LN_EPS).then_inc(s_v.h, 1)
                v.drain()
            s_v.inc()
            VL = s_v.n

            @blk.scalar
            def _(sc):
                sc.wait_ge(s_v.h, VL)
                sc.sqrt(rowbuf[0:1, 1024:1024 + w], rowbuf[0:1, 512:512 + w]).then_inc(s_s.h, 1)
            s_s.inc()
            SL3 = s_s.n

            @blk.vector
            def _(v):
                v.wait_ge(s_s.h, SL3)
                v.reciprocal(rowbuf[0:1, 1024:1024 + w], rowbuf[0:1, 1024:1024 + w])
                v.drain()
                v.scalar_tensor_tensor(rowbuf[0:1, 1536:1536 + w], rowbuf[0:1, 0:w], -1.0,
                                       rowbuf[0:1, 1024:1024 + w], ALU.mult,
                                       ALU.mult).then_inc(s_v.h, 1)
                v.drain()
            s_v.inc()
            VL2 = s_v.n

            @blk.tensor
            def _(te):
                te.wait_ge(s_v.h, VL2)
                te.matmul(ps_b[:, 0:w], c.ones_row_f[0:1, :], rowbuf[0:1, 1024:1024 + w],
                          start=True, stop=True)
                te.matmul(ps_b[:, 512:512 + w], c.ones_row_f[0:1, :], rowbuf[0:1, 1536:1536 + w],
                          start=True, stop=True).then_inc(s_t.h, 1)
            s_t.inc()
            TL3 = s_t.n

            @blk.vector
            def _(v):
                v.wait_ge(s_t.h, TL3)
                for ic in range(16):
                    v.tensor_mul(xout[:, ic, osub], x_res[:, ic, sub], ps_b[:, 0:w])
                v.drain()
                for ic in range(16):
                    ins = v.tensor_tensor(xout[:, ic, osub], xout[:, ic, osub],
                                          ps_b[:, 512:512 + w], ALU.add)
                    if ic == 15:
                        ins.then_inc(s_v.h, 1)
                v.drain()
            s_v.inc()
    c.V_LN = s_v.n


def _l0ffn(c, use):
    nc, blk, t = c.nc, c.blk, c.t
    s_t, s_v, s_s, s_ev, s_dw, s_cc = c.s_t, c.s_v, c.s_s, c.s_ev, c.s_dw, c.s_cc
    x_res, rowbuf = c.x_res, c.rowbuf
    gu_all = t["gu_all"].rearrange("s (ot p x) -> (s ot) p x", ot=11, p=128)
    dn_all = t["dnw_all"].rearrange("s (ot p x) -> (s ot) p x", ot=2, p=128)
    arena_v = c.arena[:].rearrange("p (a b) -> p a b", a=16)

    with nc.sbuf_tensor(f"l0xln{use}", [128, 16, 512], BF16) as xln, \
         nc.sbuf_tensor(f"l0h{use}", [128, 44, 512], BF16) as hbuf, \
         nc.sbuf_tensor(f"l0y{use}", [128, 16, 512], BF16) as ybuf, \
         nc.sbuf_tensor(f"l0sg{use}", [128, 2, 512], BF16) as sgbuf, \
         nc.sbuf_tensor(f"l0ws{use}", [128, 4, 2048], BF16) as wstrip4, \
         nc.psum_tensor(f"l0pA{use}", [128, 512], F32) as psA, \
         nc.psum_tensor(f"l0pB{use}", [128, 512], F32) as psB, \
         nc.psum_tensor(f"l0pC{use}", [128, 512], F32) as psC, \
         nc.psum_tensor(f"l0pD{use}", [128, 512], F32) as psD:
        wstrip = [wstrip4[:, k, :] for k in range(4)]
        psg = [psA, psC]
        psu = [psB, psD]

        @blk.sync
        def _(sy):
            sy.wait_ge(s_cc.h, c.CC_PREA)

        for half in range(2):
            cols = slice(512 * half, 512 * (half + 1))
            pw = (s_ev.h, c.EV_XLOADED) if (use == 1 and half == 0) else None
            _layernorm(c, cols, xln, ocols=slice(0, 512), pre_wait=pw)
            VLN = c.V_LN

            EV0 = s_ev.n
            TPRE = s_t.n
            wtgt = {}

            @blk.sync
            def _(sy):
                sy.wait_ge(s_t.h, TPRE)
                for i in range(44):
                    for gu in range(2):
                        k = 2 * i + gu
                        mt = i + 44 * gu
                        sidx, ot = divmod(mt, 11)
                        if i >= 2 and gu == 0:
                            sy.wait_ge(s_ev.h, EV0 + 2 * (i - 1))
                        sy.dma_start(out=wstrip[k % 4],
                                     in_=gu_all[11 * sidx + ot]).then_inc(c.s_w[k % 4].h, 16)
                        wtgt[k] = c.s_w[k % 4].inc(16)

            for i in range(44):
                @blk.tensor
                def _(te):
                    if i == 0:
                        te.wait_ge(s_v.h, VLN)
                    te.wait_ge(c.s_w[(2 * i) % 4].h, wtgt[2 * i])
                    te.wait_ge(c.s_w[(2 * i + 1) % 4].h, wtgt[2 * i + 1])
                    if i >= 2:
                        te.wait_ge(s_ev.h, EV0 + 2 * (i - 1))
                    for ic in range(16):
                        te.matmul(psg[i % 2][:], wstrip[(2 * i) % 4][:, 128 * ic:128 * (ic + 1)],
                                  xln[:, ic, :], start=(ic == 0), stop=(ic == 15))
                    for ic in range(16):
                        ins = te.matmul(psu[i % 2][:], wstrip[(2 * i + 1) % 4][:, 128 * ic:128 * (ic + 1)],
                                        xln[:, ic, :], start=(ic == 0), stop=(ic == 15))
                        if ic == 15:
                            ins.then_inc(s_t.h, 1)
                s_t.inc()
                TG = s_t.n

                @blk.scalar
                def _(sc):
                    sc.wait_ge(s_t.h, TG)
                    if i >= 2:
                        sc.wait_ge(s_ev.h, EV0 + 2 * (i - 1))
                    sc.activation(sgbuf[:, i % 2, :], psg[i % 2][:], AF.Silu).then_inc(s_s.h, 1)
                s_s.inc()
                SG = s_s.n

                @blk.vector
                def _(v):
                    v.wait_ge(s_s.h, SG)
                    v.tensor_mul(hbuf[:, i, :], sgbuf[:, i % 2, :], psu[i % 2][:]).then_inc(s_ev.h, 2)
                s_ev.inc(2)

            HD = s_ev.n

            TD0 = s_t.n
            dtgt = {}

            @blk.sync
            def _(sy):
                sy.wait_ge(s_t.h, TD0)
                for m in range(16):
                    for q in range(3):
                        k = 3 * m + q
                        if k >= 4:
                            sy.wait_ge(s_t.h, TD0 + (k - 4) // 3 + 1)
                        lo, hi = 2048 * q, min(2048 * (q + 1), 5632)
                        sy.dma_start(out=wstrip[k % 4][:, 0:hi - lo],
                                     in_=dn_all[m][:, lo:hi]).then_inc(c.s_w[k % 4].h, 16)
                        dtgt[k] = c.s_w[k % 4].inc(16)

            for m in range(16):
                @blk.tensor
                def _(te):
                    for q in range(3):
                        k = 3 * m + q
                        te.wait_ge(c.s_w[k % 4].h, dtgt[k])
                    if m == 0:
                        te.wait_ge(s_ev.h, HD)
                    if m >= 2:
                        te.wait_ge(s_ev.h, HD + m - 1)
                    for kc in range(44):
                        q, r = divmod(kc, 16)
                        ins = te.matmul(psg[m % 2][:], wstrip[(3 * m + q) % 4][:, 128 * r:128 * (r + 1)],
                                        hbuf[:, kc, :], start=(kc == 0), stop=(kc == 43))
                        if kc == 43:
                            ins.then_inc(s_t.h, 1)
                s_t.inc()
                TD = s_t.n

                @blk.scalar
                def _(sc):
                    sc.wait_ge(s_t.h, TD)
                    sc.copy(ybuf[:, m, :], psg[m % 2][:]).then_inc(s_ev.h, 1)
                s_ev.inc()
            YD = s_ev.n

            with nc.psum_tensor(f"l0rq{use}{half}", [1, 512], F32) as ps_q, \
                 nc.psum_tensor(f"l0rb{use}{half}", [128, 512], F32) as ps_b:
                VA = s_v.n

                @blk.scalar
                def _(sc):
                    sc.wait_ge(s_ev.h, YD)
                    sc.wait_ge(s_v.h, VA)
                    for ic in range(16):
                        ins = sc.square(arena_v[:, ic, 0:512], ybuf[:, ic, :])
                        if ic == 15:
                            ins.then_inc(s_s.h, 1)
                s_s.inc()
                SR = s_s.n

                @blk.tensor
                def _(te):
                    te.wait_ge(s_s.h, SR)
                    for ic in range(16):
                        ins = te.matmul(ps_q[0:1, :], c.ones_bf[:, 0:1], arena_v[:, ic, 0:512],
                                        start=(ic == 0), stop=(ic == 15))
                        if ic == 15:
                            ins.then_inc(s_t.h, 1)
                s_t.inc()
                TR = s_t.n

                @blk.vector
                def _(v):
                    v.wait_ge(s_t.h, TR)
                    v.tensor_scalar(rowbuf[0:1, 0:512], ps_q[0:1, :], 1.0 / 2048.0, L0_EPS,
                                    ALU.mult, ALU.add).then_inc(s_v.h, 1)
                    v.drain()
                s_v.inc()
                VR = s_v.n

                @blk.scalar
                def _(sc):
                    sc.wait_ge(s_v.h, VR)
                    sc.sqrt(rowbuf[0:1, 512:1024], rowbuf[0:1, 0:512]).then_inc(s_s.h, 1)
                s_s.inc()
                SR2 = s_s.n

                @blk.vector
                def _(v):
                    v.wait_ge(s_s.h, SR2)
                    v.reciprocal(rowbuf[0:1, 512:1024], rowbuf[0:1, 512:1024])
                    v.drain()
                    v.tensor_scalar_mul(rowbuf[0:1, 512:1024], rowbuf[0:1, 512:1024],
                                        0.5).then_inc(s_v.h, 1)
                    v.drain()
                s_v.inc()
                VR2 = s_v.n

                @blk.tensor
                def _(te):
                    te.wait_ge(s_v.h, VR2)
                    te.matmul(ps_b[:], c.ones_row_f[0:1, :], rowbuf[0:1, 512:1024],
                              start=True, stop=True).then_inc(s_t.h, 1)
                s_t.inc()
                TR2 = s_t.n

                @blk.vector
                def _(v):
                    v.wait_ge(s_t.h, TR2)
                    for ic in range(16):
                        v.tensor_mul(arena_v[:, ic, 0:512], ybuf[:, ic, :], ps_b[:])
                    v.drain()
                    for ic in range(16):
                        ins = v.tensor_tensor(x_res[:, ic, cols], x_res[:, ic, cols],
                                              arena_v[:, ic, 0:512], ALU.add)
                        if ic == 15:
                            ins.then_inc(s_v.h, 1)
                    v.drain()
                s_v.inc()


def _mamba(c):
    nc, blk, t = c.nc, c.blk, c.t
    s_dw, s_dx, s_dg = c.s_dw, c.s_dx, c.s_dg
    s_t, s_v, s_s, s_cc, s_ev = c.s_t, c.s_v, c.s_s, c.s_cc, c.s_ev

    win_all = t["win_all"].rearrange("s (ot p x) -> (s ot) p x", ot=8, p=128)
    wout_all = t["wout_all"].rearrange("s (ot p x) -> (s ot) p x", ot=2, p=128)
    hin_v = t["h_a2a_in"].rearrange("s (j p x) -> (s j) p x", j=4, p=128)
    hout_v = t["h_a2a_out"].rearrange("s (e p x) -> (s e) p x", e=4, p=128)
    yin_v = t["y_a2a_in"].rearrange("s (e p x) -> (s e) p x", e=4, p=128)
    yout_v = t["y_a2a_out"].rearrange("s (e p x) -> (s e) p x", e=4, p=128)
    sg_v = t["sg_d"].rearrange("one (ic p x) -> (one ic) p x", ic=32, p=128)
    ssm_my_v = t["ssm_my"].rearrange("one (r x) -> (one r) x", r=160)
    ssm_all_v = t["ssm_all"].rearrange("s (r x) -> s r x", r=160)
    xh_my_v = t["xhalo_my"].rearrange("one (ic p k) -> p (one ic) k", ic=16, p=128)
    xh_all_v = t["xhalo_all"].rearrange("s (ic p k) -> s p ic k", ic=16, p=128)

    with nc.sbuf_tensor("m_dtwT", [128, 4, 128], F32) as dtwT, \
         nc.sbuf_tensor("m_xpT", [128, 32, 160], BF16) as xpT:

        with nc.sbuf_tensor("m_xps", [128, 4096], F32) as xps, \
             nc.sbuf_tensor("m_dts", [128, 4, 128], F32) as dts:
            @blk.scalar
            def _(sc):
                sc.dma_start(out=xps[:], in_=t["xproj"][0:128, :]).then_inc(s_dx.h, 16)
                sc.dma_start(out=dts[:], in_=t["dtw_sh"].rearrange("(j p) r -> p j r", p=128)).then_inc(s_dx.h, 16)
            s_dx.inc(32)
            DX0 = s_dx.n

            @blk.tensor
            def _(te):
                te.wait_ge(s_dx.h, DX0)
            items = [(xps[:, 128 * ic:128 * (ic + 1)], xpT[:, ic, 0:128]) for ic in range(32)]
            items += [(dts[:, j, :], dtwT[:, j, :]) for j in range(4)]
            c.transpose_stream("m0a", items, F32)
            EV0 = s_ev.n

            @blk.scalar
            def _(sc):
                sc.wait_ge(s_ev.h, EV0)
                sc.dma_start(out=xps[0:32, :], in_=t["xproj"][128:160, :]).then_inc(s_dx.h, 16)
            s_dx.inc(16)
            DX1 = s_dx.n
            with nc.psum_tensor("m_pT32", [128, 128], F32) as pT32:
                for ic in range(32):
                    TB = s_t.n
                    EB = s_ev.n

                    @blk.tensor
                    def _(te):
                        if ic == 0:
                            te.wait_ge(s_dx.h, DX1)
                        else:
                            te.wait_ge(s_ev.h, EB)
                        te.transpose(pT32[:, 0:32], xps[0:32, 128 * ic:128 * (ic + 1)],
                                     c.ident_f[0:32, 0:32]).then_inc(s_t.h, 1)
                    s_t.inc()
                    TB1 = s_t.n

                    @blk.scalar
                    def _(sc):
                        sc.wait_ge(s_t.h, TB1)
                        sc.copy(xpT[:, ic, 128:160], pT32[:, 0:32]).then_inc(s_ev.h, 1)
                    s_ev.inc()

        c.fence()
        ns_m1 = nc.named_scope("m1"); ns_m1.__enter__()
        with nc.sbuf_tensor("m_xm", [128, 16, 1024], BF16) as xm, \
             nc.sbuf_tensor("m_hrow", [128, 1056], F32) as hrow, \
             nc.sbuf_tensor("m_cvacc", [128, 1024], F32) as cvacc, \
             nc.sbuf_tensor("m_hcv", [128, 2, 1024], BF16) as hcv, \
             nc.sbuf_tensor("m_sgb", [128, 1024], BF16) as sgb, \
             nc.sbuf_tensor("m_xh4", [128, 4, 16, 3], BF16) as xh4, \
             nc.sbuf_tensor("m_xhsb", [128, 16, 3], BF16) as xh_sb, \
             nc.sbuf_tensor("m_gsel", [128, 4], F32) as gsel, \
             nc.sbuf_tensor("m_wst", [128, 2, 2048], BF16) as wst, \
             nc.sbuf_tensor("m_ssb", [128, 1024], F32) as ssm_sb, \
             nc.sbuf_tensor("m_sbc", [32, 1024], F32) as ssm_bc:

            @blk.sync
            def _(sy):
                sy.wait_ge(s_cc.h, c.CC_PREB)

            _layernorm(c, slice(0, 512), xm, ocols=slice(0, 512))
            _layernorm(c, slice(512, 1024), xm, ocols=slice(512, 1024))
            VLN = c.V_LN

            @blk.scalar
            def _(sc):
                sc.wait_ge(s_v.h, VLN)
                with nc.allow_non_contiguous_dma(reason="tiny halo"):
                    sc.dma_start(out=xh_my_v, in_=xm[:, :, 1021:1024]).then_inc(s_dx.h, 16)
                sc.dma_start(out=gsel[:], in_=t["gsel"]).then_inc(s_dx.h, 16)
            s_dx.inc(32)
            DXH = s_dx.n

            @blk.gpsimd
            def _(g):
                g.wait_ge(s_dx.h, DXH)
                g.collective_compute("AllGather", ALU.bypass, replica_groups=GRP4,
                                     ins=[t["xhalo_my"]], outs=[t["xhalo_all"]]).then_inc(s_cc.h, 1)
            s_cc.inc()
            CCH = s_cc.n

            @blk.scalar
            def _(sc):
                sc.wait_ge(s_cc.h, CCH)
                with nc.allow_non_contiguous_dma(reason="tiny halo"):
                    for s in range(4):
                        sc.dma_start(out=xh4[:, s, :, :], in_=xh_all_v[s]).then_inc(s_dx.h, 16)
            s_dx.inc(64)
            DXH2 = s_dx.n

            @blk.vector
            def _(v):
                v.wait_ge(s_dx.h, DXH2)
                v.tensor_scalar(xh_sb[:].rearrange("p a k -> p (a k)"),
                                xh4[:, 0, :, :].rearrange("p a k -> p (a k)"),
                                gsel[:, 0:1], None, ALU.mult)
                v.drain()
                for s in range(1, 4):
                    v.scalar_tensor_tensor(xh_sb[:].rearrange("p a k -> p (a k)"),
                                           xh4[:, s, :, :].rearrange("p a k -> p (a k)"),
                                           gsel[:, s:s + 1],
                                           xh_sb[:].rearrange("p a k -> p (a k)"),
                                           ALU.mult, ALU.add)
                    v.drain()
                v.engine_nop().then_inc(s_v.h, 1)
            s_v.inc()
            VXH = s_v.n

            ps_stack = contextlib.ExitStack()
            pA = ps_stack.enter_context(nc.psum_tensor("m_pA", [128, 512], F32))
            pB = ps_stack.enter_context(nc.psum_tensor("m_pB", [128, 512], F32))
            pH = ps_stack.enter_context(nc.psum_tensor("m_pH", [128, 8], F32))
            pS1a = ps_stack.enter_context(nc.psum_tensor("m_pS1a", [128, 512], F32))
            pS1b = ps_stack.enter_context(nc.psum_tensor("m_pS1b", [128, 512], F32))
            pS2a = ps_stack.enter_context(nc.psum_tensor("m_pS2a", [32, 512], F32))
            pS2b = ps_stack.enter_context(nc.psum_tensor("m_pS2b", [32, 512], F32))
            V_CV = [s_v.n]
            S_SI = [s_s.n]
            G_DH = [s_dg.n]
            T_XP = [s_t.n]
            T0 = s_t.n
            tmm = {}
            sevac = {}
            mtgt = {}

            def inproj_mark(o):
                return T0 + 2 * o + 1 if o < 32 else T0 + 64 + (o - 32) + 1

            @blk.sync
            def _(sy):
                for ot in range(64):
                    if ot >= 2:
                        sy.wait_ge(s_t.h, inproj_mark(ot - 2))
                    sy.dma_start(out=wst[:, ot % 2, :],
                                 in_=win_all[ot]).then_inc(c.s_w[ot % 2].h, 16)
                    mtgt[ot] = c.s_w[ot % 2].inc(16)

            sil = {}
            xpm = {}
            ghm = {}

            def emit_xproj(o):
                @blk.tensor
                def _(te):
                    te.wait_ge(s_s.h, sil[o])
                    te.matmul(pS1a[:], xpT[:, o, 0:128], hcv[:, o % 2, 0:512],
                              start=(o == 0), stop=(o == 31))
                    te.matmul(pS1b[:], xpT[:, o, 0:128], hcv[:, o % 2, 512:1024],
                              start=(o == 0), stop=(o == 31))
                    te.matmul(pS2a[:], xpT[:, o, 128:160], hcv[:, o % 2, 0:512],
                              start=(o == 0), stop=(o == 31))
                    te.matmul(pS2b[:], xpT[:, o, 128:160], hcv[:, o % 2, 512:1024],
                              start=(o == 0), stop=(o == 31)).then_inc(s_t.h, 1)
                s_t.inc()
                xpm[o] = s_t.n
                T_XP[0] = s_t.n

            for ot in range(64):
                hpath = ot < 32
                if 1 <= ot <= 31:
                    emit_xproj(ot - 1)

                @blk.tensor
                def _(te):
                    te.wait_ge(c.s_w[ot % 2].h, mtgt[ot])
                    if ot == 0:
                        te.wait_ge(s_v.h, VXH)
                        te.wait_ge(s_v.h, VLN)
                    if ot >= 1:
                        te.wait_ge(s_s.h, sevac[ot - 1])
                    for ic in range(16):
                        te.matmul(pA[:], wst[:, ot % 2, 128 * ic:128 * (ic + 1)],
                                  xm[:, ic, 0:512], start=(ic == 0), stop=(ic == 15))
                    for ic in range(16):
                        ins = te.matmul(pB[:], wst[:, ot % 2, 128 * ic:128 * (ic + 1)],
                                        xm[:, ic, 512:1024], start=(ic == 0), stop=(ic == 15))
                        if not hpath and ic == 15:
                            ins.then_inc(s_t.h, 1)
                    if hpath:
                        for ic in range(16):
                            ins = te.matmul(pH[:, 0:3], wst[:, ot % 2, 128 * ic:128 * (ic + 1)],
                                            xh_sb[:, ic, :], start=(ic == 0), stop=(ic == 15))
                            if ic == 15:
                                ins.then_inc(s_t.h, 1)
                s_t.inc()
                tmm[ot] = s_t.n

                if hpath:
                    @blk.scalar
                    def _(sc):
                        sc.wait_ge(s_t.h, tmm[ot])
                        if ot >= 1:
                            sc.wait_ge(s_v.h, V_CV[0])
                        sc.copy(hrow[:, 0:3], pH[:, 0:3])
                        sc.copy(hrow[:, 3:515], pA[:])
                        sc.copy(hrow[:, 515:1027], pB[:]).then_inc(s_s.h, 1)
                    s_s.inc()
                    sevac[ot] = s_s.n

                    @blk.vector
                    def _(v):
                        v.wait_ge(s_s.h, sevac[ot])
                        if ot >= 1:
                            v.wait_ge(s_s.h, S_SI[0])
                        v.tensor_scalar(cvacc[:], hrow[:, 0:1024], c.convw_c[:, ot, 0:1],
                                        None, ALU.mult)
                        v.drain()
                        for k in range(1, 4):
                            v.scalar_tensor_tensor(cvacc[:], hrow[:, k:k + 1024],
                                                   c.convw_c[:, ot, k:k + 1], cvacc[:],
                                                   ALU.mult, ALU.add)
                            v.drain()
                        v.tensor_scalar(cvacc[:], cvacc[:], c.convb_c[:, ot:ot + 1],
                                        None, ALU.add).then_inc(s_v.h, 1)
                        v.drain()
                    s_v.inc()
                    V_CV[0] = s_v.n

                    @blk.scalar
                    def _(sc):
                        sc.wait_ge(s_v.h, V_CV[0])
                        if ot >= 2:
                            sc.wait_ge(s_t.h, xpm[ot - 2])
                            sc.wait_ge(c.s_w[4 + ot % 2].h, ghm[ot - 2])
                        sc.activation(hcv[:, ot % 2, :], cvacc[:], AF.Silu).then_inc(s_s.h, 1)
                    s_s.inc()
                    S_SI[0] = s_s.n
                    sil[ot] = s_s.n

                    @blk.gpsimd
                    def _(g):
                        g.wait_ge(s_s.h, S_SI[0])
                        g.dma_start(out=hin_v[ot],
                                    in_=hcv[:, ot % 2, :]).then_inc(c.s_w[4 + ot % 2].h, 16)
                    ghm[ot] = c.s_w[4 + ot % 2].inc(16)
                else:
                    @blk.scalar
                    def _(sc):
                        sc.wait_ge(s_t.h, tmm[ot])
                        if ot >= 33:
                            sc.wait_ge(s_dg.h, G_DH[0])
                        sc.activation(sgb[:, 0:512], pA[:], AF.Silu)
                        sc.activation(sgb[:, 512:1024], pB[:], AF.Silu).then_inc(s_s.h, 1)
                    s_s.inc()
                    sevac[ot] = s_s.n

                    @blk.gpsimd
                    def _(g):
                        g.wait_ge(s_s.h, sevac[ot])
                        g.dma_start(out=sg_v[ot - 32], in_=sgb[:]).then_inc(s_dg.h, 16)
                    s_dg.inc(16)
                    G_DH[0] = s_dg.n

                if ot == 31:
                    emit_xproj(31)
                    TXPF = T_XP[0]

                    @blk.scalar
                    def _(sc):
                        sc.wait_ge(s_t.h, TXPF)
                        sc.copy(ssm_sb[:, 0:512], pS1a[:])
                        sc.copy(ssm_sb[:, 512:1024], pS1b[:])
                        sc.copy(ssm_bc[:, 0:512], pS2a[:])
                        sc.copy(ssm_bc[:, 512:1024], pS2b[:]).then_inc(s_s.h, 1)
                    s_s.inc()
                    SSM1 = s_s.n

                    @blk.scalar
                    def _(sc):
                        sc.wait_ge(s_s.h, SSM1)
                        sc.dma_start(out=ssm_my_v[0:128, :], in_=ssm_sb[:]).then_inc(s_dx.h, 16)
                        sc.dma_start(out=ssm_my_v[128:160, :], in_=ssm_bc[:]).then_inc(s_dx.h, 16)
                    s_dx.inc(32)
                    DSSM = s_dx.n

                    @blk.gpsimd
                    def _(g):
                        g.wait_ge(s_dx.h, DSSM)
                        g.collective_compute("AllGather", ALU.bypass, replica_groups=ALL8,
                                             ins=[t["ssm_my"]], outs=[t["ssm_all"]]).then_inc(s_cc.h, 1)
                        g.wait_ge(c.s_w[4].h, ghm[30])
                        g.wait_ge(c.s_w[5].h, ghm[31])
                        g.collective_compute("AllToAll", ALU.bypass, replica_groups=ALL8,
                                             ins=[t["h_a2a_in"]], outs=[t["h_a2a_out"]]).then_inc(s_cc.h, 1)
                    s_cc.inc(2)
            ps_stack.close()
            CC_M1 = s_cc.n

        ns_m1.__exit__(None, None, None)
        c.fence()
        ns_m2 = nc.named_scope("m2"); ns_m2.__enter__()
        with nc.sbuf_tensor("m_sel", [32, 32, 128], BF16) as sel32, \
             nc.sbuf_tensor("m_bcb", [32, 1024], BF16) as bcb, \
             nc.sbuf_tensor("m_dt", [128, 4, 1024], F32) as dt, \
             nc.sbuf_tensor("m_hj", [128, 4, 1024], BF16) as hj, \
             nc.sbuf_tensor("m_dth", [128, 4, 1024], BF16) as dth, \
             nc.sbuf_tensor("m_yac", [128, 4, 1024], F32) as yac, \
             nc.sbuf_tensor("m_dA", [128, 4, 1024], F32) as dA, \
             nc.sbuf_tensor("m_dBu", [128, 2, 1024], F32) as dBu, \
             nc.sbuf_tensor("m_st", [128, 2, 1024], F32) as stt_, \
             nc.sbuf_tensor("m_carry", [128, 64], F32) as carry, \
             nc.sbuf_tensor("m_dtr", [128, 1024], F32) as dtr, \
             nc.sbuf_tensor("m_bc32", [32, 1024], F32) as bc32, \
             nc.psum_tensor("m_pbc0", [128, 1024], F32) as pbc0, \
             nc.psum_tensor("m_pcc0", [128, 1024], F32) as pcc0, \
             nc.psum_tensor("m_pbc1", [128, 1024], F32) as pbc1, \
             nc.psum_tensor("m_pcc1", [128, 1024], F32) as pcc1:
            pbcs = [pbc0, pbc1]
            pccs = [pcc0, pcc1]

            @blk.sync
            def _(sy):
                sy.wait_ge(s_cc.h, CC_M1)

            @blk.scalar
            def _(sc):
                sc.dma_start(out=sel32[:],
                             in_=t["sel32"].rearrange("k (j m) -> k j m", j=32)).then_inc(s_dx.h, 16)
            s_dx.inc(16)

            T_LAST = [s_t.n]
            V_LAST = [s_v.n]
            G_LAST = [s_dg.n]
            for j in range(8):
                TJ = T_LAST[0]
                VJ = V_LAST[0]
                GJ = G_LAST[0]

                @blk.scalar
                def _(sc):
                    sc.wait_ge(s_t.h, TJ)
                    sc.wait_ge(s_v.h, VJ)
                    sc.wait_ge(s_dg.h, GJ)
                    sc.dma_start(out=dtr[:], in_=ssm_all_v[j][0:128, :]).then_inc(s_dx.h, 16)
                    sc.dma_start(out=bc32[:], in_=ssm_all_v[j][128:160, :]).then_inc(s_dx.h, 16)
                    for eh in range(4):
                        sc.dma_start(out=hj[:, eh, :], in_=hout_v[4 * j + eh]).then_inc(s_dx.h, 16)
                s_dx.inc(96)
                DJ = s_dx.n

                @blk.vector
                def _(v):
                    v.wait_ge(s_dx.h, DJ)
                    v.tensor_copy(bcb[:], bc32[:]).then_inc(s_v.h, 1)
                    v.drain()
                s_v.inc()
                VBC = s_v.n

                vm = {}
                for eh in range(4):
                    pd = [pbc0, pcc0, pbc1, pcc1][eh]

                    @blk.tensor
                    def _(te):
                        if eh == 0:
                            te.wait_ge(s_dx.h, DJ)
                        if j > 0:
                            te.wait_ge(s_v.h, vjprev[0 if eh < 2 else 1])
                        te.matmul(pd[:, 0:512], dtwT[:, eh, :], dtr[:, 0:512],
                                  start=True, stop=True)
                        te.matmul(pd[:, 512:1024], dtwT[:, eh, :], dtr[:, 512:1024],
                                  start=True, stop=True).then_inc(s_t.h, 1)
                    s_t.inc()
                    TDT = s_t.n

                    @blk.vector
                    def _(v):
                        v.wait_ge(s_t.h, TDT)
                        v.tensor_scalar(dt[:, eh, :], pd[:], c.dtb_c[:, eh:eh + 1],
                                        None, ALU.add).then_inc(s_v.h, 1)
                        v.drain()
                    s_v.inc()
                    vm[eh] = s_v.n
                VDT = s_v.n

                @blk.scalar
                def _(sc):
                    sc.wait_ge(s_v.h, VDT)
                    for eh in range(4):
                        sc.activation(dt[:, eh, :], dt[:, eh, :], AF.Exp)
                    sc.drain()
                    for eh in range(4):
                        ins = sc.activation(dt[:, eh, :], dt[:, eh, :], AF.Ln, bias=1.0)
                        if eh == 3:
                            ins.then_inc(s_s.h, 1)
                    sc.drain()
                s_s.inc()
                SDT = s_s.n

                @blk.vector
                def _(v):
                    v.wait_ge(s_s.h, SDT)
                    v.wait_ge(s_dx.h, DJ)
                    for eh in range(4):
                        v.tensor_mul(dth[:, eh, :], dt[:, eh, :], hj[:, eh, :])
                    for eh in range(4):
                        ins = v.tensor_scalar(yac[:, eh, :], hj[:, eh, :],
                                              c.dcol_c[:, eh:eh + 1], None, ALU.mult)
                        if eh == 3:
                            ins.then_inc(s_v.h, 1)
                    v.drain()
                s_v.inc()
                VPRE = s_v.n

                vdone = {}
                sA = {}
                for n in range(16):
                    pbc = pbcs[n % 2]
                    pcc = pccs[n % 2]

                    @blk.tensor
                    def _(te):
                        if n >= 2:
                            te.wait_ge(s_v.h, vdone[n - 2])
                        elif n < 2:
                            te.wait_ge(s_v.h, vm[2 * n + 1])
                        if n == 0:
                            te.wait_ge(s_v.h, VBC)
                        te.matmul(pbc[:, 0:512], sel32[:, n, :], bcb[:, 0:512],
                                  start=True, stop=True)
                        te.matmul(pbc[:, 512:1024], sel32[:, n, :], bcb[:, 512:1024],
                                  start=True, stop=True)
                        te.matmul(pcc[:, 0:512], sel32[:, 16 + n, :], bcb[:, 0:512],
                                  start=True, stop=True)
                        te.matmul(pcc[:, 512:1024], sel32[:, 16 + n, :], bcb[:, 512:1024],
                                  start=True, stop=True).then_inc(s_t.h, 1)
                    s_t.inc()
                    TBC = s_t.n

                    for pp in range(2):
                        gp = 2 * n + pp
                        EVA = s_ev.n
                        ehs = (2 * pp, 2 * pp + 1)
                        sl = (gp % 2) * 2

                        @blk.scalar
                        def _(sc):
                            if gp >= 2:
                                sc.wait_ge(s_ev.h, EVA - 1)
                            if gp == 0:
                                sc.wait_ge(s_s.h, SDT)
                            sc.activation(dA[:, sl, :], dt[:, ehs[0], :], AF.Exp,
                                          scale=c.aneg_c[:, ehs[0], n:n + 1])
                            sc.activation(dA[:, sl + 1, :], dt[:, ehs[1], :], AF.Exp,
                                          scale=c.aneg_c[:, ehs[1], n:n + 1]).then_inc(s_s.h, 1)
                        s_s.inc()
                        sA[gp] = s_s.n

                        @blk.vector
                        def _(v):
                            v.wait_ge(s_s.h, sA[gp])
                            if pp == 0:
                                v.wait_ge(s_t.h, TBC)
                            for k in range(2):
                                v.tensor_mul(dBu[:, k, :], dth[:, ehs[k], :], pbc[:])
                            v.drain()
                            for k in range(2):
                                idx = 16 * ehs[k] + n
                                if j % 4 == 0:
                                    v.tensor_tensor_scan(stt_[:, k, :], dA[:, sl + k, :],
                                                         dBu[:, k, :], 0.0, ALU.mult, ALU.add)
                                else:
                                    v.tensor_tensor_scan(stt_[:, k, :], dA[:, sl + k, :],
                                                         dBu[:, k, :], carry[:, idx:idx + 1],
                                                         ALU.mult, ALU.add)
                            v.drain()
                            for k in range(2):
                                idx = 16 * ehs[k] + n
                                ins = v.tensor_copy(carry[:, idx:idx + 1], stt_[:, k, 1023:1024])
                                if k == 0:
                                    ins.then_inc(s_ev.h, 1)
                            for k in range(2):
                                v.tensor_mul(dBu[:, k, :], stt_[:, k, :], pcc[:])
                            v.drain()
                            for k in range(2):
                                ins = v.tensor_tensor(yac[:, ehs[k], :], yac[:, ehs[k], :],
                                                      dBu[:, k, :], ALU.add)
                                if k == 1:
                                    ins.then_inc(s_v.h, 1)
                            v.drain()
                        s_v.inc()
                        s_ev.inc()
                    vdone[n] = s_v.n
                vjprev = {0: vdone[14], 1: vdone[15]}

                VY = s_v.n

                @blk.vector
                def _(v):
                    v.wait_ge(s_v.h, VY)
                    for eh in range(4):
                        ins = v.tensor_copy(hj[:, eh, :], yac[:, eh, :])
                        if eh == 3:
                            ins.then_inc(s_v.h, 1)
                    v.drain()
                s_v.inc()
                V_LAST[0] = s_v.n
                VYC = s_v.n

                @blk.gpsimd
                def _(g):
                    g.wait_ge(s_v.h, VYC)
                    for eh in range(4):
                        g.dma_start(out=yin_v[4 * j + eh], in_=hj[:, eh, :]).then_inc(s_dg.h, 16)
                s_dg.inc(64)
                G_LAST[0] = s_dg.n
                T_LAST[0] = s_t.n

            GY = s_dg.n

            @blk.gpsimd
            def _(g):
                g.wait_ge(s_dg.h, GY)
                g.collective_compute("AllToAll", ALU.bypass, replica_groups=ALL8,
                                     ins=[t["y_a2a_in"]], outs=[t["y_a2a_out"]]).then_inc(s_cc.h, 1)
            s_cc.inc()
            CC_M2 = s_cc.n

        ns_m2.__exit__(None, None, None)
        c.fence()
        ns_m3 = nc.named_scope("m3"); ns_m3.__enter__()
        with nc.sbuf_tensor("m_yfin", [128, 32, 1024], BF16) as yfin, \
             nc.sbuf_tensor("m_ytl", [128, 1024], BF16) as ytl, \
             nc.sbuf_tensor("m_sgt", [128, 1024], BF16) as sgt, \
             nc.sbuf_tensor("m_wso", [128, 2, 4096], BF16) as wso, \
             nc.psum_tensor("m_pO0", [128, 512], F32) as pO0, \
             nc.psum_tensor("m_pO1", [128, 512], F32) as pO1:

            @blk.sync
            def _(sy):
                sy.wait_ge(s_cc.h, CC_M2)

            VFM = [s_v.n]
            for ic in range(32):
                VF = VFM[0]

                @blk.scalar
                def _(sc):
                    sc.wait_ge(s_v.h, VF)
                    sc.dma_start(out=ytl[:], in_=yout_v[ic]).then_inc(s_dx.h, 16)
                    sc.dma_start(out=sgt[:], in_=sg_v[ic]).then_inc(s_dx.h, 16)
                s_dx.inc(32)
                DYF = s_dx.n

                @blk.vector
                def _(v):
                    v.wait_ge(s_dx.h, DYF)
                    v.tensor_mul(yfin[:, ic, :], ytl[:], sgt[:]).then_inc(s_v.h, 1)
                    v.drain()
                s_v.inc()
                VFM[0] = s_v.n
            VFIN = s_v.n

            TO0 = s_t.n
            otgt = {}

            @blk.sync
            def _(sy):
                for ht in range(16):
                    if ht >= 2:
                        sy.wait_ge(s_t.h, TO0 + ht - 1)
                    sy.dma_start(out=wso[:, ht % 2, :],
                                 in_=wout_all[ht]).then_inc(c.s_w[ht % 2].h, 16)
                    otgt[ht] = c.s_w[ht % 2].inc(16)

            VOE = [s_v.n]
            for ht in range(16):
                @blk.tensor
                def _(te):
                    te.wait_ge(c.s_w[ht % 2].h, otgt[ht])
                    if ht == 0:
                        te.wait_ge(s_v.h, VFIN)
                    if ht >= 1:
                        te.wait_ge(s_v.h, VOE[0])
                    for ic in range(32):
                        te.matmul(pO0[:], wso[:, ht % 2, 128 * ic:128 * (ic + 1)],
                                  yfin[:, ic, 0:512], start=(ic == 0), stop=(ic == 31))
                    for ic in range(32):
                        ins = te.matmul(pO1[:], wso[:, ht % 2, 128 * ic:128 * (ic + 1)],
                                        yfin[:, ic, 512:1024], start=(ic == 0), stop=(ic == 31))
                        if ic == 31:
                            ins.then_inc(s_t.h, 1)
                s_t.inc()
                TMO = s_t.n

                @blk.vector
                def _(v):
                    v.wait_ge(s_t.h, TMO)
                    v.tensor_tensor(c.x_res[:, ht, 0:512], c.x_res[:, ht, 0:512],
                                    pO0[:], ALU.add)
                    v.tensor_tensor(c.x_res[:, ht, 512:1024], c.x_res[:, ht, 512:1024],
                                    pO1[:], ALU.add).then_inc(s_v.h, 1)
                    v.drain()
                s_v.inc()
                VOE[0] = s_v.n
        ns_m3.__exit__(None, None, None)


def _l1ffn(c, use):
    nc, blk, t = c.nc, c.blk, c.t
    s_dw, s_dx, s_dg = c.s_dw, c.s_dx, c.s_dg
    s_t, s_v, s_s, s_cc, s_ev = c.s_t, c.s_v, c.s_s, c.s_cc, c.s_ev

    wgq_v = t["wgq_all"].rearrange("s (ic p x) -> (s ic) p x", ic=16, p=128)
    wuq_v = t["wuq_all"].rearrange("s (ic p x) -> (s ic) p x", ic=16, p=128)
    wdq_v = t["wdq_all"].rearrange("s (ot p x) -> (s ot) p x", ot=2, p=128)
    hd_v = t["h_d"].rearrange("one (tt p x) -> (one tt) p x", tt=8, p=128)
    hqd_v = t["hq_d"].rearrange("one (ic p x) -> (one ic) p x", ic=64, p=128)
    hqd_pv = t["hq_d"].rearrange("one (ic p x) -> p (one ic) x", ic=64, p=128)

    with nc.sbuf_tensor(f"f_cols{use}", [128, 48], F32) as cols, \
         nc.sbuf_tensor(f"f_rbh{use}", [128, 1024], F32) as RBH, \
         nc.sbuf_tensor(f"f_hsr{use}", [1, 1024], F32) as hsrow:
        xq_stack = contextlib.ExitStack()
        xq = xq_stack.enter_context(nc.sbuf_tensor(f"f_xq{use}", [128, 16, 1024], BF16))

        @blk.sync
        def _(sy):
            sy.wait_ge(s_cc.h, c.CC_PREB)

        ns_fq = nc.named_scope(f"fq{use}"); ns_fq.__enter__()
        with nc.sbuf_tensor(f"f_xln{use}", [128, 16, 256], F32) as xlnf, \
             nc.sbuf_tensor(f"f_xt{use}", [128, 2048], F32) as xt, \
             nc.sbuf_tensor(f"f_qi{use}", [128, 2048], I32) as qi, \
             nc.sbuf_tensor(f"f_qc{use}", [128, 2048], F32) as qc, \
             nc.sbuf_tensor(f"f_e1{use}", [128, 2048], F32) as e1, \
             nc.sbuf_tensor(f"f_xqt{use}", [128, 2048], BF16) as xqt:
            for qd in range(4):
                _layernorm(c, slice(256 * qd, 256 * (qd + 1)), xlnf,
                           ocols=slice(0, 256), w=256)
                VLN = c.V_LN
                for tt in range(2):
                    gt = 2 * qd + tt
                    VXT = s_v.n

                    @blk.tensor
                    def _(te):
                        te.wait_ge(s_v.h, VLN)

                    @blk.scalar
                    def _(sc):
                        sc.wait_ge(s_v.h, VXT)
                    items = [(xlnf[:, ic, 128 * tt:128 * (tt + 1)],
                              xt[:, 128 * ic:128 * (ic + 1)]) for ic in range(16)]
                    c.transpose_stream(f"fq{use}{gt}", items, F32)
                    EVQ = s_ev.n

                    @blk.vector
                    def _(v):
                        v.wait_ge(s_ev.h, EVQ)
                        v.reduce_max(cols[:, 24 + gt:25 + gt], xt[:], AX.X,
                                     apply_absolute_value=True)
                        v.drain()
                        v.tensor_scalar(cols[:, gt:gt + 1], cols[:, 24 + gt:25 + gt],
                                        1.0 / 127.0, L1_EPS, ALU.mult, ALU.add)
                        v.drain()
                        v.reciprocal(cols[:, 24 + gt:25 + gt], cols[:, gt:gt + 1])
                        v.drain()
                        v.tensor_scalar(xt[:], xt[:], cols[:, 24 + gt:25 + gt],
                                        None, ALU.mult)
                        v.drain()
                        c.emit_round(v, xqt, xt, qi, qc, e1, 2048)
                        v.engine_nop().then_inc(s_v.h, 1)
                    s_v.inc()
                    VRQ = s_v.n

                    @blk.tensor
                    def _(te):
                        te.wait_ge(s_v.h, VRQ)
                    items = [(xqt[:, 128 * ic:128 * (ic + 1)],
                              xq[:, ic, 128 * gt:128 * (gt + 1)]) for ic in range(16)]
                    c.transpose_stream(f"fqb{use}{gt}", items, BF16)
            VXS0 = s_v.n

            @blk.vector
            def _(v):
                v.wait_ge(s_v.h, VXS0)
                v.wait_ge(s_v.h, c.V_WS)
                v.tensor_scalar(cols[:, 8:16], cols[:, 0:8], c.ws_c[:, 0:1], None, ALU.mult)
                v.tensor_scalar(cols[:, 16:24], cols[:, 0:8], c.ws_c[:, 1:2],
                                None, ALU.mult).then_inc(s_v.h, 1)
                v.drain()
            s_v.inc()
            VXS = s_v.n

        ns_fq.__exit__(None, None, None)
        c.fence()
        ns_fg = nc.named_scope(f"fgu{use}"); ns_fg.__enter__()
        EVXQ = s_ev.n
        with nc.sbuf_tensor(f"f_wgu{use}", [128, 2, 16, 512], BF16) as wsgu, \
             nc.sbuf_tensor(f"f_gf{use}", [128, 8, 512], F32) as gf, \
             nc.sbuf_tensor(f"f_uf{use}", [128, 2, 512], F32) as uf, \
             nc.sbuf_tensor(f"f_hbs{use}", [128, 512], BF16) as hbs, \
             nc.psum_tensor(f"f_pG0{use}", [128, 512], F32) as pG0, \
             nc.psum_tensor(f"f_pG1{use}", [128, 512], F32) as pG1:
            psp = [pG0, pG1]
            T0 = s_t.n
            gtgt = {}
            utgt = {}

            @blk.sync
            def _(sy):
                for orange in range(16):
                    s, oh = orange // 2, orange % 2
                    if orange >= 1:
                        sy.wait_ge(s_t.h, T0 + 16 * (orange - 1) + 8)
                    for ic in range(16):
                        sy.dma_start(out=wsgu[:, 0, ic, :],
                                     in_=wgq_v[s * 16 + ic][:, 512 * oh:512 * (oh + 1)]).then_inc(c.s_w[0].h, 16)
                        c.s_w[0].inc(16)
                    gtgt[orange] = c.s_w[0].n
                    if orange >= 1:
                        sy.wait_ge(s_t.h, T0 + 16 * orange)
                    for ic in range(16):
                        sy.dma_start(out=wsgu[:, 1, ic, :],
                                     in_=wuq_v[s * 16 + ic][:, 512 * oh:512 * (oh + 1)]).then_inc(c.s_w[1].h, 16)
                        c.s_w[1].inc(16)
                    utgt[orange] = c.s_w[1].n

            SEV = [s_s.n]
            VEV = [s_v.n]
            GEV = [s_dg.n]
            for orange in range(16):
                sgm = {}
                for tt in range(8):
                    @blk.tensor
                    def _(te):
                        if orange == 0 and tt == 0:
                            te.wait_ge(s_ev.h, EVXQ)
                        te.wait_ge(c.s_w[0].h, gtgt[orange])
                        if tt >= 2:
                            te.wait_ge(s_s.h, sgm[tt - 2])
                        elif orange >= 1:
                            te.wait_ge(s_v.h, VEV[0])
                        for ic in range(16):
                            ins = te.matmul(psp[tt % 2][:], xq[:, ic, 128 * tt:128 * (tt + 1)],
                                            wsgu[:, 0, ic, :], start=(ic == 0), stop=(ic == 15))
                            if ic == 15:
                                ins.then_inc(s_t.h, 1)
                    s_t.inc()
                    TGM = s_t.n

                    @blk.scalar
                    def _(sc):
                        sc.wait_ge(s_t.h, TGM)
                        if orange == 0 and tt == 0:
                            sc.wait_ge(s_v.h, VXS)
                        if orange >= 1 and tt == 0:
                            sc.wait_ge(s_v.h, VEV[0])
                        sc.activation(gf[:, tt, :], psp[tt % 2][:], AF.Silu,
                                      scale=cols[:, 8 + tt:9 + tt]).then_inc(s_s.h, 1)
                    s_s.inc()
                    sgm[tt] = s_s.n
                SEV[0] = s_s.n

                sum_ = {}
                for tt in range(8):
                    @blk.tensor
                    def _(te):
                        te.wait_ge(c.s_w[1].h, utgt[orange])
                        if tt >= 2:
                            te.wait_ge(s_v.h, sum_[tt - 2])
                        else:
                            te.wait_ge(s_s.h, sgm[6 + tt])
                        for ic in range(16):
                            ins = te.matmul(psp[tt % 2][:], xq[:, ic, 128 * tt:128 * (tt + 1)],
                                            wsgu[:, 1, ic, :], start=(ic == 0), stop=(ic == 15))
                            if ic == 15:
                                ins.then_inc(s_t.h, 1)
                    s_t.inc()
                    TUM = s_t.n
                    GP = GEV[0]

                    @blk.scalar
                    def _(sc):
                        sc.wait_ge(s_t.h, TUM)
                        if tt >= 2:
                            sc.wait_ge(s_v.h, sum_[tt - 2])
                        sc.activation(uf[:, tt % 2, :], psp[tt % 2][:], AF.Copy,
                                      scale=cols[:, 16 + tt:17 + tt]).then_inc(s_s.h, 1)
                    s_s.inc()
                    SUF = s_s.n

                    @blk.vector
                    def _(v):
                        v.wait_ge(s_s.h, SUF)
                        v.wait_ge(s_s.h, sgm[tt])
                        v.wait_ge(s_dg.h, GP)
                        v.tensor_mul(hbs[:], gf[:, tt, :],
                                     uf[:, tt % 2, :]).then_inc(s_v.h, 1)
                        v.drain()
                    s_v.inc()
                    sum_[tt] = s_v.n
                    VH = s_v.n

                    @blk.gpsimd
                    def _(g):
                        g.wait_ge(s_v.h, VH)
                        g.dma_start(out=hd_v[tt][:, 512 * orange:512 * (orange + 1)],
                                    in_=hbs[:]).then_inc(s_dg.h, 16)
                    s_dg.inc(16)
                    GEV[0] = s_dg.n
                VEV[0] = s_v.n

        xq_stack.close()
        ns_fg.__exit__(None, None, None)
        c.fence()
        ns_fh = nc.named_scope(f"fhq{use}"); ns_fh.__enter__()
        GUD = s_dg.n
        with nc.sbuf_tensor(f"f_hbt{use}", [128, 8192], BF16) as hbt, \
             nc.sbuf_tensor(f"f_vt{use}", [128, 2048], F32) as vt, \
             nc.sbuf_tensor(f"f_qi2{use}", [128, 2048], I32) as qi, \
             nc.sbuf_tensor(f"f_qc2{use}", [128, 2048], F32) as qc, \
             nc.sbuf_tensor(f"f_e2{use}", [128, 2048], F32) as e1, \
             nc.sbuf_tensor(f"f_hqt{use}", [128, 8192], BF16) as hqt, \
             nc.sbuf_tensor(f"f_hqf{use}", [128, 8192], BF16) as hqf, \
             nc.sbuf_tensor(f"f_hs8{use}", [8, 128], F32) as hsr8:
            for tt in range(8):
                VHT = s_v.n
                EVH = s_ev.n
                GHD = s_dg.n

                @blk.scalar
                def _(sc):
                    if tt == 0:
                        sc.wait_ge(s_dg.h, GUD)
                    sc.wait_ge(s_v.h, VHT)
                    sc.dma_start(out=hbt[:], in_=hd_v[tt]).then_inc(s_dx.h, 16)
                s_dx.inc(16)
                DHT = s_dx.n

                @blk.vector
                def _(v):
                    v.wait_ge(s_dx.h, DHT)
                    v.reduce_max(cols[:, 32 + tt:33 + tt], hbt[:], AX.X,
                                 apply_absolute_value=True)
                    v.drain()
                    v.tensor_scalar(cols[:, 32 + tt:33 + tt], cols[:, 32 + tt:33 + tt],
                                    1.0 / 127.0, L1_EPS, ALU.mult, ALU.add)
                    v.drain()
                    v.engine_nop().then_inc(s_v.h, 1)
                s_v.inc()
                VHS = s_v.n

                @blk.vector
                def _(v):
                    v.reciprocal(cols[:, 40:41], cols[:, 32 + tt:33 + tt])
                    v.drain()
                    for ch in range(4):
                        S = slice(2048 * ch, 2048 * (ch + 1))
                        v.tensor_scalar(vt[:], hbt[:, S], cols[:, 40:41], None, ALU.mult)
                        v.drain()
                        c.emit_round(v, hqt[:, S], vt, qi, qc, e1, 2048)
                    v.engine_nop().then_inc(s_v.h, 1)
                s_v.inc()
                VHQ = s_v.n

                @blk.tensor
                def _(te):
                    te.wait_ge(s_v.h, VHQ)

                @blk.scalar
                def _(sc):
                    sc.wait_ge(s_dg.h, GHD)
                items = [(hqt[:, 128 * ic:128 * (ic + 1)], hqf[:, 128 * ic:128 * (ic + 1)])
                         for ic in range(64)]
                c.transpose_stream(f"fh{use}{tt}", items, BF16)
                EVT = s_ev.n

                @blk.gpsimd
                def _(g):
                    g.wait_ge(s_ev.h, EVT)
                    with nc.allow_non_contiguous_dma(reason="hq scatter"):
                        for g8 in range(8):
                            g.dma_start(
                                out=hqd_pv[:, 8 * g8:8 * (g8 + 1), 128 * tt:128 * (tt + 1)],
                                in_=hqf[:, 1024 * g8:1024 * (g8 + 1)].rearrange(
                                    "p (a b) -> p a b", a=8)).then_inc(s_dg.h, 16)
                s_dg.inc(16 * 8)

            VHR = s_v.n
            EVF = s_ev.n
            with nc.psum_tensor(f"f_pT{use}", [128, 128], F32) as pT, \
                 nc.psum_tensor(f"f_pR{use}", [128, 1024], F32) as pR:
                @blk.tensor
                def _(te):
                    te.wait_ge(s_v.h, VHR)
                    te.wait_ge(s_ev.h, EVF)
                    te.transpose(pT[0:8, :], cols[:, 32:40], c.ident_f[:]).then_inc(s_t.h, 1)
                s_t.inc()
                THS = s_t.n

                @blk.scalar
                def _(sc):
                    sc.wait_ge(s_t.h, THS)
                    sc.copy(hsr8[:], pT[0:8, :]).then_inc(s_s.h, 1)
                s_s.inc()
                SHS = s_s.n

                @blk.scalar
                def _(sc):
                    sc.wait_ge(s_s.h, SHS)
                    with nc.allow_non_contiguous_dma(reason="hs row gather"):
                        sc.dma_start(out=hsrow[:], in_=hsr8[:]).then_inc(s_dx.h, 16)
                s_dx.inc(16)
                DHS = s_dx.n

                @blk.tensor
                def _(te):
                    te.wait_ge(s_dx.h, DHS)
                    te.matmul(pR[:, 0:512], c.ones_row_f[0:1, :], hsrow[0:1, 0:512],
                              start=True, stop=True)
                    te.matmul(pR[:, 512:1024], c.ones_row_f[0:1, :], hsrow[0:1, 512:1024],
                              start=True, stop=True).then_inc(s_t.h, 1)
                s_t.inc()
                TRB = s_t.n

                @blk.vector
                def _(v):
                    v.wait_ge(s_t.h, TRB)
                    v.tensor_copy(RBH[:], pR[:]).then_inc(s_v.h, 1)
                    v.drain()
                s_v.inc()
            VRBH = s_v.n

        ns_fh.__exit__(None, None, None)
        c.fence()
        ns_fd = nc.named_scope(f"fdn{use}"); ns_fd.__enter__()
        with nc.sbuf_tensor(f"f_wdn{use}", [128, 4, 8192], BF16) as wstr, \
             nc.sbuf_tensor(f"f_hq2{use}", [128, 2, 1024], BF16) as hq2, \
             nc.sbuf_tensor(f"f_t1{use}", [128, 1024], F32) as t1, \
             nc.psum_tensor(f"f_pD0{use}", [128, 1024], F32) as pD0, \
             nc.psum_tensor(f"f_pD1{use}", [128, 1024], F32) as pD1, \
             nc.psum_tensor(f"f_pD2{use}", [128, 1024], F32) as pD2, \
             nc.psum_tensor(f"f_pD3{use}", [128, 1024], F32) as pD3:
            psD = [pD0, pD1, pD2, pD3]
            GHQ = s_dg.n
            DWD = s_dw.n
            TD0 = s_t.n
            VDE = [s_v.n]

            @blk.sync
            def _(sy):
                sy.wait_ge(s_dg.h, GHQ)
                for grp in range(4):
                    if grp >= 1:
                        sy.wait_ge(s_t.h, TD0 + 64 * grp)
                    for k in range(4):
                        sy.dma_start(out=wstr[:, k, :],
                                     in_=wdq_v[4 * grp + k]).then_inc(s_dw.h, 16)
            s_dw.inc(16 * 16)

            for grp in range(4):
                dhq = {}

                @blk.scalar
                def _(sc):
                    sc.wait_ge(s_dg.h, GHQ)
                for ic in range(64):
                    TIC = TD0 + 64 * grp + ic - 1

                    @blk.scalar
                    def _(sc):
                        if ic >= 2:
                            sc.wait_ge(s_t.h, TIC)
                        elif grp >= 1:
                            sc.wait_ge(s_t.h, TD0 + 64 * grp - 1 + ic)
                        sc.dma_start(out=hq2[:, ic % 2, :],
                                     in_=hqd_v[ic]).then_inc(c.s_w[2 + ic % 2].h, 16)
                    dhq[ic] = c.s_w[2 + ic % 2].inc(16)

                    @blk.tensor
                    def _(te):
                        te.wait_ge(c.s_w[2 + ic % 2].h, dhq[ic])
                        te.wait_ge(s_dw.h, DWD + 16 * (4 * grp + 4))
                        if grp >= 1 and ic == 0:
                            te.wait_ge(s_v.h, VDE[0])
                        for k in range(4):
                            te.matmul(psD[k][:, 0:512], wstr[:, k, 128 * ic:128 * (ic + 1)],
                                      hq2[:, ic % 2, 0:512], start=(ic == 0), stop=(ic == 63))
                            ins = te.matmul(psD[k][:, 512:1024], wstr[:, k, 128 * ic:128 * (ic + 1)],
                                            hq2[:, ic % 2, 512:1024], start=(ic == 0), stop=(ic == 63))
                            if k == 3:
                                ins.then_inc(s_t.h, 1)
                    s_t.inc()
                TDN = s_t.n

                @blk.vector
                def _(v):
                    v.wait_ge(s_t.h, TDN)
                    if grp == 0:
                        v.wait_ge(s_v.h, VRBH)
                    for k in range(4):
                        ho = 4 * grp + k
                        v.tensor_mul(t1[:], psD[k][:], RBH[:])
                        v.drain()
                        v.tensor_scalar(t1[:], t1[:], c.ws_c[:, 7:8], None, ALU.mult)
                        v.drain()
                        ins = v.tensor_tensor(c.x_res[:, ho, :], c.x_res[:, ho, :],
                                              t1[:], ALU.add)
                        if k == 3:
                            ins.then_inc(s_v.h, 1)
                        v.drain()
                s_v.inc()
                VDE[0] = s_v.n
        ns_fd.__exit__(None, None, None)


def _attn(c):
    nc, blk, t = c.nc, c.blk, c.t
    s_dw, s_dx, s_dg = c.s_dw, c.s_dx, c.s_dg
    s_t, s_v, s_s, s_cc, s_ev = c.s_t, c.s_v, c.s_s, c.s_cc, c.s_ev

    kv_my_v = t["kv_my"].rearrange("one (k p x) -> (one k) p x", k=8, p=128)
    kv_all_v = t["kv_all"].rearrange("s (k p x) -> (s k) p x", k=8, p=128)

    with nc.sbuf_tensor("a_Q", [128, 16, 1024], BF16) as Qfm, \
         nc.sbuf_tensor("a_K", [128, 4, 1024], BF16) as Kfm, \
         nc.sbuf_tensor("a_V", [128, 8, 512], BF16) as Vtm, \
         nc.sbuf_tensor("a_gsel", [128, 4], F32) as gsel, \
         nc.sbuf_tensor("a_pm", [128, 1], F32) as pmcol:

        @blk.scalar
        def _(sc):
            sc.dma_start(out=gsel[:], in_=t["gsel"]).then_inc(s_dx.h, 16)
            sc.dma_start(out=pmcol[:], in_=t["pmask"]).then_inc(s_dx.h, 16)
        s_dx.inc(32)
        DSEL = s_dx.n

        ns_ap = nc.named_scope("a_proj"); ns_ap.__enter__()
        with nc.sbuf_tensor("a_xa", [128, 16, 1024], BF16) as xa:
            kv_stack = contextlib.ExitStack()
            wkv = kv_stack.enter_context(nc.sbuf_tensor("a_wkv", [128, 16, 512], BF16))
            wcs = kv_stack.enter_context(nc.sbuf_tensor("a_wcs", [128, 2, 512], F32))

            _layernorm(c, slice(0, 512), xa, ocols=slice(0, 512))
            _layernorm(c, slice(512, 1024), xa, ocols=slice(512, 1024))
            VLN = c.V_LN

            with nc.psum_tensor("a_pk0", [128, 512], F32) as pk0, \
                 nc.psum_tensor("a_pk1", [128, 512], F32) as pk1:
                sev = [s_s.n]
                tkv = [s_t.n]
                for wnm in ("wk_d", "wv_d"):
                    dkv = {}
                    kcast = {}

                    def emit_kv_dma(ti):
                        @blk.scalar
                        def _(sc):
                            if wnm == "wv_d" and ti == 0:
                                sc.wait_ge(s_t.h, tkv[0])
                            if ti >= 2:
                                sc.wait_ge(s_s.h, kcast[ti - 2])
                            sc.dma_start(out=wcs[:, ti % 2, :],
                                         in_=t[wnm][128 * ti:128 * (ti + 1), :]).then_inc(c.s_w[ti % 2].h, 16)
                        dkv[ti] = c.s_w[ti % 2].inc(16)

                    def emit_kv_cast(ti):
                        @blk.scalar
                        def _(sc):
                            sc.wait_ge(c.s_w[ti % 2].h, dkv[ti])
                            sc.copy(wkv[:, ti, :], wcs[:, ti % 2, :]).then_inc(s_s.h, 1)
                        s_s.inc()
                        kcast[ti] = s_s.n

                    for ti in range(16):
                        emit_kv_dma(ti)
                        if ti >= 1:
                            emit_kv_cast(ti - 1)
                    emit_kv_cast(15)
                    SKV = s_s.n

                    if wnm == "wk_d":
                        for mk in range(4):
                            @blk.tensor
                            def _(te):
                                if mk == 0:
                                    te.wait_ge(s_s.h, SKV)
                                    te.wait_ge(s_v.h, VLN)
                                else:
                                    te.wait_ge(s_s.h, sev[0])
                                for ic in range(16):
                                    te.matmul(pk0[:], wkv[:, ic, 128 * mk:128 * (mk + 1)],
                                              xa[:, ic, 0:512], start=(ic == 0), stop=(ic == 15))
                                for ic in range(16):
                                    ins = te.matmul(pk1[:], wkv[:, ic, 128 * mk:128 * (mk + 1)],
                                                    xa[:, ic, 512:1024], start=(ic == 0), stop=(ic == 15))
                                    if ic == 15:
                                        ins.then_inc(s_t.h, 1)
                            s_t.inc()
                            TK = s_t.n
                            tkv[0] = s_t.n

                            @blk.scalar
                            def _(sc):
                                sc.wait_ge(s_t.h, TK)
                                sc.copy(Kfm[:, mk, 0:512], pk0[:])
                                sc.copy(Kfm[:, mk, 512:1024], pk1[:]).then_inc(s_s.h, 1)
                            s_s.inc()
                            sev[0] = s_s.n
                    else:
                        for tt in range(8):
                            @blk.tensor
                            def _(te):
                                te.wait_ge(s_s.h, sev[0])
                                if tt == 0:
                                    te.wait_ge(s_s.h, SKV)
                                for ic in range(16):
                                    ins = te.matmul(pk0[:], xa[:, ic, 128 * tt:128 * (tt + 1)],
                                                    wkv[:, ic, :], start=(ic == 0), stop=(ic == 15))
                                    if ic == 15:
                                        ins.then_inc(s_t.h, 1)
                            s_t.inc()
                            TV = s_t.n

                            @blk.scalar
                            def _(sc):
                                sc.wait_ge(s_t.h, TV)
                                sc.copy(Vtm[:, tt, :], pk0[:]).then_inc(s_s.h, 1)
                            s_s.inc()
                            sev[0] = s_s.n
            SKVD = s_s.n

            @blk.gpsimd
            def _(g):
                g.wait_ge(s_s.h, SKVD)
                for gk in range(4):
                    g.dma_start(out=kv_my_v[gk], in_=Kfm[:, gk, :]).then_inc(s_dg.h, 16)
                for u in range(4):
                    g.dma_start(out=kv_my_v[4 + u],
                                in_=Vtm[:, 2 * u:2 * u + 2, :]).then_inc(s_dg.h, 16)
            s_dg.inc(16 * 8)
            GKV = s_dg.n

            @blk.gpsimd
            def _(g):
                g.wait_ge(s_dg.h, GKV)
                g.collective_compute("AllGather", ALU.bypass, replica_groups=GRP4,
                                     ins=[t["kv_my"]], outs=[t["kv_all"]]).then_inc(s_cc.h, 1)
            s_cc.inc()
            CCKV = s_cc.n
            c.fence()
            kv_stack.close()

            with nc.sbuf_tensor("a_wqs", [128, 2, 16, 128], F32) as wqs, \
                 nc.sbuf_tensor("a_wqb", [128, 2, 16, 128], BF16) as wqb, \
                 nc.psum_tensor("a_pq0", [128, 512], F32) as pq0, \
                 nc.psum_tensor("a_pq1", [128, 512], F32) as pq1:
                casts = {}
                mms = {}
                dq = {}
                SQ0 = s_s.n

                @blk.sync
                def _(sy):
                    for mt in range(16):
                        if mt >= 2:
                            sy.wait_ge(s_s.h, SQ0 + 2 * (mt - 2) + 1)
                        with nc.allow_non_contiguous_dma(reason="wq col slice"):
                            sy.dma_start(out=wqs[:, mt % 2, :, :],
                                         in_=t["wq_d"][:, 128 * mt:128 * (mt + 1)].rearrange(
                                             "(ti p) m -> p ti m", p=128)).then_inc(c.s_w[mt % 2].h, 16)
                        dq[mt] = c.s_w[mt % 2].inc(16)

                for mt in range(16):
                    @blk.scalar
                    def _(sc):
                        sc.wait_ge(c.s_w[mt % 2].h, dq[mt])
                        if mt >= 2:
                            sc.wait_ge(s_t.h, mms[mt - 2])
                        sc.copy(wqb[:, mt % 2, :, :], wqs[:, mt % 2, :, :]).then_inc(s_s.h, 1)
                    s_s.inc()
                    casts[mt] = s_s.n
                    SQ = s_s.n

                    @blk.tensor
                    def _(te):
                        te.wait_ge(s_s.h, SQ)
                        for ic in range(16):
                            te.matmul(pq0[:], wqb[:, mt % 2, ic, :], xa[:, ic, 0:512],
                                      start=(ic == 0), stop=(ic == 15))
                        for ic in range(16):
                            ins = te.matmul(pq1[:], wqb[:, mt % 2, ic, :], xa[:, ic, 512:1024],
                                            start=(ic == 0), stop=(ic == 15))
                            if ic == 15:
                                ins.then_inc(s_t.h, 1)
                    s_t.inc()
                    mms[mt] = s_t.n
                    TQ2 = s_t.n

                    @blk.scalar
                    def _(sc):
                        sc.wait_ge(s_t.h, TQ2)
                        sc.activation(Qfm[:, mt, 0:512], pq0[:], AF.Copy, scale=ISQD)
                        sc.activation(Qfm[:, mt, 512:1024], pq1[:], AF.Copy,
                                      scale=ISQD).then_inc(s_s.h, 1)
                    s_s.inc()

        c.fence()
        kvp_stack = contextlib.ExitStack()
        Kprev = kvp_stack.enter_context(nc.sbuf_tensor("a_Kp", [128, 4, 1024], BF16))
        Vprev = kvp_stack.enter_context(nc.sbuf_tensor("a_Vp", [128, 8, 512], BF16))
        with nc.sbuf_tensor("a_kvt", [128, 1024], BF16) as kvt:
            @blk.sync
            def _(sy):
                sy.wait_ge(s_cc.h, CCKV)

            VCB = [s_v.n]
            Kp_f = Kprev[:].rearrange("p a b -> p (a b)")
            Vp_f = Vprev[:].rearrange("p a b -> p (a b)")
            for slot in range(8):
                dstv = Kp_f[:, 1024 * slot:1024 * (slot + 1)] if slot < 4 else \
                    Vp_f[:, 1024 * (slot - 4):1024 * (slot - 3)]
                for s in range(4):
                    VC = VCB[0]

                    @blk.scalar
                    def _(sc):
                        sc.wait_ge(s_cc.h, CCKV)
                        if s > 0 or slot > 0:
                            sc.wait_ge(s_v.h, VC)
                        sc.dma_start(out=kvt[:], in_=kv_all_v[8 * s + slot]).then_inc(s_dx.h, 16)
                    s_dx.inc(16)
                    DCB = s_dx.n

                    @blk.vector
                    def _(v):
                        v.wait_ge(s_dx.h, DCB)
                        if s == 0:
                            v.tensor_scalar(dstv, kvt[:], gsel[:, 0:1], None, ALU.mult)
                        else:
                            v.scalar_tensor_tensor(dstv, kvt[:], gsel[:, s:s + 1], dstv,
                                                   ALU.mult, ALU.add)
                        v.drain()
                        v.engine_nop().then_inc(s_v.h, 1)
                    s_v.inc()
                    VCB[0] = s_v.n
        VKV = s_v.n

        ns_ap.__exit__(None, None, None)
        c.fence()
        ns_as = nc.named_scope("a_sm"); ns_as.__enter__()
        with nc.sbuf_tensor("a_scf", [128, 2, 1152], F32) as scf, \
             nc.sbuf_tensor("a_P", [128, 2, 1152], BF16) as Pb, \
             nc.sbuf_tensor("a_PT", [128, 2, 9, 128], BF16) as PnT, \
             nc.sbuf_tensor("a_mx", [128, 8], F32) as mxc, \
             nc.psum_tensor("a_psS", [128, 1536], F32) as psS, \
             nc.psum_tensor("a_psA", [128, 128], F32) as psA:
            VS = [s_v.n]
            SE = [s_ev.n]
            combos = [(4 * g + qh, g, qt) for g in range(4) for qh in range(4)
                      for qt in range(8)]
            mk = {}

            def emit_scores(idx):
                hd, g, qt = combos[idx]
                nprev = 8 - qt

                @blk.tensor
                def _(te):
                    if idx == 0:
                        te.wait_ge(s_v.h, VKV)
                    te.wait_ge(s_v.h, VS[0])
                    pieces = []
                    w = 0
                    while w < 9:
                        nn = min(4 - (w % 4), (nprev - w) if w < nprev else 9 - w)
                        if w < nprev:
                            rhs = Kprev[:, g, 128 * (qt + w):128 * (qt + w + nn)]
                        else:
                            rhs = Kfm[:, g, 128 * (qt + w - 8):128 * (qt + w - 8 + nn)]
                        pieces.append((128 * w, rhs, nn))
                        w += nn
                    for i, (col0, rhs, nn) in enumerate(pieces):
                        ins = te.matmul(psS[:, col0:col0 + 128 * nn],
                                        Qfm[:, hd, 128 * qt:128 * (qt + 1)],
                                        rhs, start=True, stop=True)
                        if i == len(pieces) - 1:
                            ins.then_inc(s_t.h, 1)
                s_t.inc()
                mk[("tsc", idx)] = s_t.n

            def emit_exp(idx):
                b = idx % 2

                @blk.scalar
                def _(sc):
                    sc.wait_ge(s_v.h, mk[("vsc", idx)])
                    if idx >= 2:
                        sc.wait_ge(s_t.h, mk[("tta", idx - 2)])
                    sc.activation(Pb[:, b, :], scf[:, b, :], AF.Exp).then_inc(s_s.h, 1)
                s_s.inc()
                mk[("spx", idx)] = s_s.n

            def emit_extract(idx):
                hd, g, qt = combos[idx]
                nprev = 8 - qt
                b = idx % 2

                @blk.scalar
                def _(sc):
                    sc.wait_ge(s_t.h, mk[("tsc", idx)])
                    sc.copy(scf[:, b, 128:1024], psS[:, 128:1024]).then_inc(s_s.h, 1)
                s_s.inc()
                mk[("smc", idx)] = s_s.n

                @blk.vector
                def _(v):
                    v.wait_ge(s_t.h, mk[("tsc", idx)])
                    v.tensor_tensor(scf[:, b, 0:128], psS[:, 0:128],
                                    c.mask_far[:], ALU.add)
                    v.tensor_tensor(scf[:, b, 1024:1152], psS[:, 1024:1152],
                                    c.mask_diag[:], ALU.add)
                    v.drain()
                    v.wait_ge(s_s.h, mk[("smc", idx)])
                    if nprev > 0:
                        v.tensor_scalar(scf[:, b, 0:128 * nprev], scf[:, b, 0:128 * nprev],
                                        pmcol[:, 0:1], None, ALU.add)
                        v.drain()
                    v.reduce_max(mxc[:, 4 * b:4 * b + 1], scf[:, b, 0:1152], AX.X)
                    v.drain()
                    v.tensor_scalar(mxc[:, 4 * b + 1:4 * b + 2], mxc[:, 4 * b:4 * b + 1],
                                    -1.0, None, ALU.mult)
                    v.drain()
                    v.tensor_scalar(scf[:, b, 0:1152], scf[:, b, 0:1152],
                                    mxc[:, 4 * b + 1:4 * b + 2],
                                    None, ALU.add).then_inc(s_v.h, 1)
                    v.drain()
                s_v.inc()
                mk[("vsc", idx)] = s_v.n
                VS[0] = s_v.n

            def emit_tail(idx):
                hd, g, qt = combos[idx]
                b = idx % 2

                @blk.vector
                def _(v):
                    v.wait_ge(s_s.h, mk[("spx", idx)])
                    v.reduce_sum(mxc[:, 4 * b + 2:4 * b + 3], Pb[:, b, :], AX.X)
                    v.drain()
                    v.reciprocal(mxc[:, 4 * b + 3:4 * b + 4], mxc[:, 4 * b + 2:4 * b + 3])
                    v.drain()
                    v.tensor_scalar(Pb[:, b, :], Pb[:, b, :], mxc[:, 4 * b + 3:4 * b + 4],
                                    None, ALU.mult).then_inc(s_v.h, 1)
                    v.drain()
                s_v.inc()
                VPN = s_v.n

                @blk.tensor
                def _(te):
                    te.wait_ge(s_v.h, VPN)
                items = [(Pb[:, b, 128 * w:128 * (w + 1)], PnT[:, b, w, :])
                         for w in range(9)]
                c.transpose_stream(f"at{hd}_{qt}", items, BF16)
                EVP = s_ev.n
                mk[("tta", idx)] = s_t.n

                @blk.tensor
                def _(te):
                    te.wait_ge(s_ev.h, EVP)
                    if idx > 0:
                        te.wait_ge(s_ev.h, SE[0])
                    for w in range(9):
                        wc = qt + w
                        if wc < 8:
                            lhsT = Vprev[:, wc, 128 * g:128 * (g + 1)]
                        else:
                            lhsT = Vtm[:, wc - 8, 128 * g:128 * (g + 1)]
                        ins = te.matmul(psA[:], lhsT, PnT[:, b, w, :],
                                        start=(w == 0), stop=(w == 8))
                        if w == 8:
                            ins.then_inc(s_t.h, 1)
                s_t.inc()
                TAT = s_t.n

                @blk.scalar
                def _(sc):
                    sc.wait_ge(s_t.h, TAT)
                    sc.copy(Qfm[:, hd, 128 * qt:128 * (qt + 1)],
                            psA[:]).then_inc(s_ev.h, 1)
                s_ev.inc()
                SE[0] = s_ev.n

            for idx in range(len(combos)):
                emit_scores(idx)
                if idx >= 1:
                    emit_exp(idx - 1)
                emit_extract(idx)
                if idx >= 1:
                    emit_tail(idx - 1)
            emit_exp(len(combos) - 1)
            emit_tail(len(combos) - 1)

        ns_as.__exit__(None, None, None)
        kvp_stack.close()
        c.fence()
        ns_ao = nc.named_scope("a_wo"); ns_ao.__enter__()
        with nc.sbuf_tensor("a_wos", [128, 2, 16, 128], F32) as wos, \
             nc.sbuf_tensor("a_wob", [128, 2, 16, 128], BF16) as wob, \
             nc.sbuf_tensor("a_t1", [128, 512], F32) as t1, \
             nc.psum_tensor("a_pW0", [128, 512], F32) as pW0, \
             nc.psum_tensor("a_pW1", [128, 512], F32) as pW1:
            vw = [s_v.n]
            SW0 = s_s.n
            TW0 = s_t.n
            dwo = {}

            @blk.sync
            def _(sy):
                for ho in range(16):
                    if ho >= 2:
                        sy.wait_ge(s_s.h, SW0 + ho - 1)
                    with nc.allow_non_contiguous_dma(reason="wo col slice"):
                        sy.dma_start(out=wos[:, ho % 2, :, :],
                                     in_=t["wo_d"][:, 128 * ho:128 * (ho + 1)].rearrange(
                                         "(ti p) m -> p ti m", p=128)).then_inc(c.s_w[2 + ho % 2].h, 16)
                    dwo[ho] = c.s_w[2 + ho % 2].inc(16)

            for ho in range(16):
                @blk.scalar
                def _(sc):
                    sc.wait_ge(c.s_w[2 + ho % 2].h, dwo[ho])
                    if ho >= 2:
                        sc.wait_ge(s_t.h, TW0 + ho - 1)
                    sc.copy(wob[:, ho % 2, :, :], wos[:, ho % 2, :, :]).then_inc(s_s.h, 1)
                s_s.inc()
                SWO = s_s.n

                @blk.tensor
                def _(te):
                    te.wait_ge(s_s.h, SWO)
                    if ho >= 1:
                        te.wait_ge(s_v.h, vw[0])
                    for ic in range(16):
                        te.matmul(pW0[:], wob[:, ho % 2, ic, :], Qfm[:, ic, 0:512],
                                  start=(ic == 0), stop=(ic == 15))
                    for ic in range(16):
                        ins = te.matmul(pW1[:], wob[:, ho % 2, ic, :], Qfm[:, ic, 512:1024],
                                        start=(ic == 0), stop=(ic == 15))
                        if ic == 15:
                            ins.then_inc(s_t.h, 1)
                s_t.inc()
                TWO = s_t.n

                @blk.vector
                def _(v):
                    v.wait_ge(s_t.h, TWO)
                    for half in range(2):
                        pw = pW0 if half == 0 else pW1
                        S = slice(512 * half, 512 * (half + 1))
                        v.tensor_scalar(t1[:], pw[:], c.bo_c[:, ho:ho + 1], None, ALU.add)
                        v.drain()
                        ins = v.tensor_tensor(c.x_res[:, ho, S], c.x_res[:, ho, S],
                                              t1[:], ALU.add)
                        if half == 1:
                            ins.then_inc(s_v.h, 1)
                        v.drain()
                s_v.inc()
                vw[0] = s_v.n
        ns_ao.__exit__(None, None, None)


def _shard_inputs(inputs):
    import ml_dtypes
    gp = np.ascontiguousarray
    x = inputs["L0_attn_hidden_states"]
    ident_bf = np.eye(128, dtype=ml_dtypes.bfloat16)
    ident_f = np.eye(128, dtype=np.float32)
    ii, jj = np.indices((128, 128))
    mask_diag = np.where(jj <= ii, 0.0, -1e30).astype(np.float32)
    mask_far = np.where(jj > ii, 0.0, -1e30).astype(np.float32)
    maps = []
    for cid in range(8):
        b, q = divmod(cid, 4)
        gsel = np.zeros((128, 4), np.float32)
        if q > 0:
            gsel[:, q - 1] = 1.0
        sel32 = np.zeros((32, 32, 128), ml_dtypes.bfloat16)
        for k in range(32):
            sel32[k, k, :] = 1.0
        sel32 = sel32.reshape(32, 32 * 128)
        pmask = np.full((128, 1), 0.0 if q > 0 else -1e30, np.float32)
        maps.append({
            "x_in": gp(x[b, 1024 * q:1024 * (q + 1), :]),
            "gsel": gsel,
            "pmask": pmask,
            "sel32": sel32,
            "ident_bf": ident_bf,
            "ident_f": ident_f,
            "mask_diag": mask_diag,
            "mask_far": mask_far,
            "gup_pk": gp(inputs["L0_ffn_gate_up_packed"][1408 * cid:1408 * (cid + 1), :]).view(np.uint8),
            "gup_sc": gp(inputs["L0_ffn_gate_up_scales"][:, 1408 * cid:1408 * (cid + 1)].T),
            "dn_pk": gp(inputs["L0_ffn_down_packed"][256 * cid:256 * (cid + 1), :]).view(np.uint8),
            "dn_sc": gp(inputs["L0_ffn_down_scales"][:, 256 * cid:256 * (cid + 1)].T),
            "wg_sh": gp(inputs["L1_ffn_w_gate"][1024 * cid:1024 * (cid + 1), :]),
            "wu_sh": gp(inputs["L1_ffn_w_up"][1024 * cid:1024 * (cid + 1), :]),
            "wd_sh": gp(inputs["L1_ffn_w_down"][256 * cid:256 * (cid + 1), :]),
            "win_sh": gp(inputs["L0_attn_in_proj_w"][1024 * cid:1024 * (cid + 1), :]),
            "wout_sh": gp(inputs["L0_attn_out_proj_w"][256 * cid:256 * (cid + 1), :]),
            "xproj": gp(inputs["L0_attn_x_proj_w"]),
            "dtw_sh": gp(inputs["L0_attn_dt_proj_w"][512 * cid:512 * (cid + 1), :]),
            "dtb_sh": gp(inputs["L0_attn_dt_proj_b"][512 * cid:512 * (cid + 1)]),
            "alog_sh": gp(inputs["L0_attn_A_log"][512 * cid:512 * (cid + 1), :]),
            "d_sh": gp(inputs["L0_attn_D"][512 * cid:512 * (cid + 1)]),
            "convw": gp(inputs["L0_attn_conv_w"].reshape(E, 4)),
            "convb": gp(inputs["L0_attn_conv_b"]),
            "wq_d": gp(inputs["L1_attn_Wq"]),
            "wk_d": gp(inputs["L1_attn_Wk"]),
            "wv_d": gp(inputs["L1_attn_Wv"]),
            "wo_d": gp(inputs["L1_attn_Wo"]),
            "bo_d": gp(inputs["L1_attn_bo"]),
        })
    return maps


def kernel(_stages=ALL_STAGES, **inputs):
    maps = _shard_inputs(inputs)
    with allow_ici_all_to_all():
        nc = build_program(stages=_stages)
        res = run_bass_kernel_spmd(nc, maps, core_ids=list(range(8)))
    out = np.empty((B, T, H), dtype=np.float32)
    for cid in range(8):
        b, q = divmod(cid, 4)
        out[b, 1024 * q:1024 * (q + 1), :] = res.results[cid]["out_x"]
    return out
